# version 19
# baseline (speedup 1.0000x reference)
import dataclasses
import math
import numpy as np
import ml_dtypes
import concourse.bass as bass
import concourse.mybir as mybir
import concourse.bass_utils as _bu
from concourse.bass_utils import run_bass_kernel_spmd

if not getattr(_bu, "_ldw_patched", False):
    _orig_run_command = _bu.run_command

    def _run_command_ldw(cmd, *a, **kw):
        if isinstance(cmd, list):
            cmd = ["--enable-ldw-opt=true" if c == "--enable-ldw-opt=false" else c
                   for c in cmd]
        return _orig_run_command(cmd, *a, **kw)

    _bu.run_command = _run_command_ldw
    _bu._ldw_patched = True

F32 = mybir.dt.float32
BF16 = mybir.dt.bfloat16
I8 = mybir.dt.int8
AF = mybir.ActivationFunctionType
OP = mybir.AluOpType

PROG_TAG = "v19"
NCORES = 8
T = 1024
D = 2048
H = 16
DCN = 16
ICN = 64
ECN = 16
KC = 64
EXP_SHIFT = -12.0
EXP_SCALE = 1.0 / math.sqrt(128.0)
EPS = 1e-5
BATCHES = [2] * 32
NB = len(BATCHES)
PVLAG = 4
PSLOT = 9
TRC_BI = 8

A_KH, A_VH, A_PT = 0, 32768, 65792
A_CS, A_SN, A_R1, A_R2 = 65792, 67840, 69888, 71936
A_SCR = 73984
A_QH = 78080
A_VS = 0
A_X2 = 0
A_XC = 65792
A_YSQ = 78080
A_F = 0
A_GU = 65792
A_OUT = 0
ARENA = 84224

ENGS = ("tensor", "vector", "scalar", "sync", "gpsimd")


def dap(base, off_elems, dims):
    return dataclasses.replace(
        base, offset=base.offset + off_elems, ap=[list(d) for d in dims]
    )


class Sched:
    def __init__(self, nc, sem_names):
        self.nc = nc
        self.sems = {n: nc.alloc_semaphore(n) for n in sem_names}
        self.cnt = {n: 0 for n in sem_names}
        self.ops = {e: [] for e in ENGS}

    def wait(self, eng, sem, val):
        if val is None or val <= 0:
            return
        s = self.sems[sem]
        self.ops[eng].append(lambda e, s=s, v=val: e.wait_ge(s, v))

    def emit(self, eng, fn, inc=None, incby=1):
        if inc is None:
            self.ops[eng].append(fn)
            return None
        s = self.sems[inc]
        self.cnt[inc] += incby
        tgt = self.cnt[inc]
        self.ops[eng].append(lambda e, fn=fn, s=s, b=incby: fn(e).then_inc(s, b))
        return tgt

    def emit_m(self, eng, fn, incs):
        tgts = {}
        for name in incs:
            self.cnt[name] += 1
            tgts[name] = self.cnt[name]
        s0 = self.sems[incs[0]]
        rest = [self.sems[n] for n in incs[1:]]

        def wrapped(e, fn=fn, s0=s0, rest=rest):
            inst = fn(e).then_inc(s0, 1)
            for s in rest:
                e.nop().then_inc(s, 1)
            return inst
        self.ops[eng].append(wrapped)
        return tgts

    def dma(self, q, out, in_, inc, **kw):
        return self.emit(
            q, lambda e, o=out, i=in_, kw=kw: e.dma_start(out=o, in_=i, **kw),
            inc=inc, incby=16)

    def run(self, eng_name, eng):
        for fn in self.ops[eng_name]:
            fn(eng)


def build_program(nphases=10, dbg=None):
    nc = bass.Bass("TRN2", target_bir_lowering=False, debug=False)

    xT_h = nc.dram_tensor("xT", [D, T], F32, kind="ExternalInput")
    wq_b = nc.dram_tensor("wq_b", [H, DCN, 128, 128], F32, kind="ExternalInput").ap()
    wk_b = nc.dram_tensor("wk_b", [H, DCN, 128, 128], F32, kind="ExternalInput").ap()
    wv_b = nc.dram_tensor("wv_b", [4, DCN, 128, 512], F32, kind="ExternalInput").ap()
    wo_b = nc.dram_tensor("wo_b", [ECN, DCN, 128, 128], F32, kind="ExternalInput").ap()
    wg_b = nc.dram_tensor("wg_b", [ICN, DCN, 128, 128], I8, kind="ExternalInput").ap()
    wu_b = nc.dram_tensor("wu_b", [ICN, DCN, 128, 128], I8, kind="ExternalInput").ap()
    wd_b = nc.dram_tensor("wd_b", [ECN, ICN, 128, 128], I8, kind="ExternalInput").ap()
    sg_i = nc.dram_tensor("sg", [128, ICN], F32, kind="ExternalInput").ap()
    su_i = nc.dram_tensor("su", [128, ICN], F32, kind="ExternalInput").ap()
    sd_i = nc.dram_tensor("sd", [128, ECN], F32, kind="ExternalInput").ap()
    cs_i = nc.dram_tensor("cs2", [128, T], BF16, kind="ExternalInput").ap()
    sn_i = nc.dram_tensor("sn2", [128, T], BF16, kind="ExternalInput").ap()
    id_i = nc.dram_tensor("ident", [128, 128], BF16, kind="ExternalInput").ap()
    ob_i = nc.dram_tensor("ones_bf", [128, 1], BF16, kind="ExternalInput").ap()
    or_i = nc.dram_tensor("ones_row", [1, 128], F32, kind="ExternalInput").ap()
    cb_i = nc.dram_tensor("cb", [128, 2], F32, kind="ExternalInput").ap()
    y_out_h = nc.dram_tensor("y_out", [D, T], F32, kind="ExternalOutput")
    dbgst = (nc.dram_tensor("dbg", [3, 1024], F32, kind="ExternalOutput")
             if dbg == "stats" else None)

    kv_loc = nc.dram_tensor("kv_loc", [4 * T, T], BF16, kind="Internal")
    kv_all = nc.dram_tensor("kv_all", [NCORES * 4 * T, T], BF16, kind="Internal",
                            addr_space="Shared")
    q_spill = nc.dram_tensor("q_spill", [D, T], BF16, kind="Internal")
    x2_spill = nc.dram_tensor("x2_spill", [D, T], BF16, kind="Internal")
    row_spill = nc.dram_tensor("row_spill", [1, 3072], F32, kind="Internal")
    xT = xT_h.ap()

    s_xb = nc.alloc_sbuf_tensor("s_xb", [128, DCN * T], BF16)
    s_xn = nc.alloc_sbuf_tensor("s_xn", [128, DCN * T], BF16)
    s_kot = nc.alloc_sbuf_tensor("s_kot", [128, H * T], BF16)
    s_wb = nc.alloc_sbuf_tensor("s_wb", [128, 2 * 4096], BF16)
    s_id = nc.alloc_sbuf_tensor("s_id", [128, 128], BF16)
    s_sg = nc.alloc_sbuf_tensor("s_sg", [128, ICN], F32)
    s_su = nc.alloc_sbuf_tensor("s_su", [128, ICN], F32)
    s_sd = nc.alloc_sbuf_tensor("s_sd", [128, ECN], F32)
    s_ones = nc.alloc_sbuf_tensor("s_ones", [128, 1], BF16)
    s_or = nc.alloc_sbuf_tensor("s_or", [1, 128], F32)
    s_cb = nc.alloc_sbuf_tensor("s_cb", [128, 2], F32)
    s_rows = nc.alloc_sbuf_tensor("s_rows", [1, 3072], F32)
    s_fl = nc.alloc_sbuf_tensor("s_fl", [128, 8], BF16)
    a_rs = nc.alloc_sbuf_tensor("a_rs", [128, 12], F32)
    nc.alloc_sbuf_tensor("tag_" + PROG_TAG, [1, 2], BF16)

    base, _aend = nc.bump_sbuf(ARENA)

    def mk_at(name, off, shape, dtype):
        return nc.alloc_sbuf_tensor_at(name, shape, dtype, offset=base + off)

    a_kh = mk_at("a_kh", A_KH, [128, 2 * 8192], BF16)
    a_vh = mk_at("a_vh", A_VH, [128, 2 * 8256], BF16)
    a_pt = mk_at("a_pt", A_PT, [128, PSLOT * 1024], BF16)
    a_cs = mk_at("a_cs", A_CS, [128, T], BF16)
    a_sn = mk_at("a_sn", A_SN, [128, T], BF16)
    a_r1 = mk_at("a_r1", A_R1, [128, T], BF16)
    a_r2 = mk_at("a_r2", A_R2, [128, T], BF16)
    a_scr = mk_at("a_scr", A_SCR, [128, T], F32)
    a_qh = mk_at("a_qh", A_QH, [128, 2 * T], BF16)
    a_vs = mk_at("a_vs", A_VS, [128, 8 * D], BF16)
    a_x2 = mk_at("a_x2", A_X2, [128, DCN * T], F32)
    a_xc = mk_at("a_xc", A_XC, [128, 2 * T], F32)
    a_ysq = mk_at("a_ysq", A_YSQ, [128, 2 * T], BF16)
    a_f = mk_at("a_f", A_F, [128, ICN * 512], BF16)
    a_gu = mk_at("a_gu", A_GU, [128, 4 * 512], BF16)
    a_out = mk_at("a_out", A_OUT, [128, 2 * T], F32)
    a_xc4 = mk_at("a_xc4", 0, [128, 4 * T], BF16)
    a_out4 = mk_at("a_out4", 16384, [128, 4 * T], F32)
    a_mb = mk_at("a_mb", 65792, [128, T], BF16)
    a_rb = mk_at("a_rb", 67840, [128, T], BF16)

    ps = nc.alloc_psum_tensor("ps", [128, 4096], F32)

    sem_names = [
        "c", "xb", "w0", "w1", "w2", "w3", "pw", "kst", "cck", "ccv",
        "kh0", "kh1", "vh0", "vh1", "qld0", "qld1", "qst0", "qst1",
        "s", "p", "pv", "an", "tr", "vc",
        "v", "a", "pe", "xc0", "xc1", "sp", "la",
        "x2l0", "x2l1", "x2l2", "x2l3", "yst0", "yst1", "yst2", "yst3",
        "dbg",
    ]
    S = Sched(nc, sem_names)

    wstate = {"wt": 0, "fam_start": 0, "nslots": 2}

    def dims_dst(src_dims):
        out = []
        stride = 1
        counts = [d[1] for d in src_dims[1:]]
        for cnt in reversed(counts):
            out.insert(0, [stride, cnt])
            stride *= cnt
        return out

    def wfamily(nslots):
        wstate["fam_start"] = wstate["wt"]
        wstate["nslots"] = nslots

    def wload(src_ap, dims, off_elems):
        wt = wstate["wt"]
        wstate["wt"] = wt + 1
        n = wstate["nslots"]
        fs = wstate["fam_start"]
        slot = (wt - fs) % n
        off = slot * (8192 // n)
        sem = "w%d" % slot
        gate = wt - n + 1 if wt - n >= fs else fs
        S.wait("gpsimd", "pw", gate)
        dst = dap(s_wb.ap(), off, [[8192, 128]] + dims_dst(dims))
        src = dap(src_ap, off_elems, dims)
        tgt = S.dma("gpsimd", dst, src, sem)
        return off, sem, tgt

    def bank(i, lo=0, hi=512):
        return ps[:, 512 * i + lo: 512 * i + hi]

    def bankrow(i, lo, hi):
        return ps[0:1, 512 * i + lo: 512 * i + hi]

    S.dma("sync", a_cs[:, :], cs_i, "c")
    S.dma("sync", a_sn[:, :], sn_i, "c")
    S.dma("sync", s_id[:, :], id_i, "c")
    S.dma("sync", s_sg[:, :], sg_i, "c")
    S.dma("sync", s_su[:, :], su_i, "c")
    S.dma("sync", s_sd[:, :], sd_i, "c")
    S.dma("sync", s_ones[:, :], ob_i, "c")
    S.dma("sync", s_or[:, :], or_i, "c")
    S.dma("sync", s_cb[:, :], cb_i, "c")
    NCONST = S.cnt["c"]
    for q4 in range(4):
        S.dma("gpsimd",
              s_xb[:, q4 * 4 * T:(q4 + 1) * 4 * T].rearrange(
                  "p (dc t) -> p dc t", dc=4),
              dap(xT, q4 * 4 * 128 * T, [[T, 128], [128 * T, 4], [1, T]]),
              "xb")
    ones_ap = dap(a_vh.ap(), 128, [[2 * 8256, 128], [8256, 2], [129, 64], [1, 1]])

    def pe_spacer(dummy_lo):
        return S.emit("tensor", lambda e: e.matmul(
            ps[0:1, dummy_lo:dummy_lo + 512], s_ones[:, :], s_xb[:, 0:512],
            start=True, stop=True), inc="pe")

    def launder_rows(c0, c1, producer):
        S.wait("sync", producer[0], producer[1])
        t1 = S.dma("sync", row_spill[0:1, c0:c1], s_rows[0:1, c0:c1], "la")
        S.wait("sync", "la", t1)
        return S.dma("sync", s_rows[0:1, c0:c1], row_spill[0:1, c0:c1], "la")

    def ln_finalize_bcast(sx_lo, sx2_lo, dummy_lo, dbg_rows=False):
        sp_t = pe_spacer(dummy_lo)
        S.wait("scalar", "pe", sp_t)
        if dbg_rows:
            for i, lo in ((0, sx_lo), (1, sx2_lo)):
                v_t0 = S.emit("vector", lambda e, lo=lo: e.tensor_copy(
                    a_scr[0:1, 0:1024], ps[0:1, lo:lo + 1024]), inc="v")
                S.wait("sync", "v", v_t0)
                d_t0 = S.dma("sync", dbgst[i:i + 1, :], a_scr[0:1, 0:1024], "dbg")
                S.wait("vector", "dbg", d_t0)
        S.emit("scalar", lambda e: e.activation(
            s_rows[0:1, 0:1024], ps[0:1, sx_lo:sx_lo + 1024],
            AF.Copy, scale=1.0 / 2048.0), inc="a")
        a_t = S.emit("scalar", lambda e: e.activation(
            s_rows[0:1, 1024:2048], ps[0:1, sx2_lo:sx2_lo + 1024],
            AF.Copy, scale=1.0 / 2048.0), inc="a")
        la1 = launder_rows(0, 2048, ("a", a_t))
        S.wait("vector", "la", la1)
        def fv(e):
            e.tensor_mul(s_rows[0:1, 2048:3072], s_rows[0:1, 0:1024],
                         s_rows[0:1, 0:1024])
            return e.tensor_sub(s_rows[0:1, 1024:2048], s_rows[0:1, 1024:2048],
                                s_rows[0:1, 2048:3072])
        v_t = S.emit("vector", fv, inc="v")
        la2 = launder_rows(1024, 2048, ("v", v_t))
        S.wait("scalar", "la", la2)
        def g(e):
            e.activation(s_rows[0:1, 2048:3072], s_rows[0:1, 1024:2048], AF.Ln,
                         bias=s_cb[0:1, 1:2])
            return e.activation(s_rows[0:1, 2048:3072], s_rows[0:1, 2048:3072],
                                AF.Exp, scale=-0.5)
        a_t2 = S.emit("scalar", g, inc="a")
        la3 = launder_rows(2048, 3072, ("a", a_t2))
        S.wait("tensor", "la", la3)
        pe_t = None
        for half in range(2):
            S.emit("tensor", lambda e, h=half: e.matmul(
                bank(4 + h), s_or[:, :], s_rows[0:1, h * 512:(h + 1) * 512],
                start=True, stop=True))
        for half in range(2):
            pe_t = S.emit("tensor", lambda e, h=half: e.matmul(
                bank(6 + h), s_or[:, :],
                s_rows[0:1, 2048 + h * 512: 2048 + (h + 1) * 512],
                start=True, stop=True), inc="pe" if half == 1 else None)
        return pe_t

    def mbrb_copy(bc_t):
        S.wait("vector", "pe", bc_t)
        S.emit("vector", lambda e: e.tensor_copy(a_mb[:, :], ps[:, 2048:3072]),
               inc="v")
        S.emit("vector", lambda e: e.tensor_copy(a_rb[:, :], ps[:, 3072:4096]),
               inc="v")
        return S.emit("vector", lambda e: e.tensor_copy(
            s_fl[:, 0:8], a_rb[:, T - 8:T]), inc="v")

    def ln_stats(src_t, first_war=None, xb_gate=False):
        sq_pe = {}
        for dc in range(DCN):
            if xb_gate and dc % 4 == 0:
                S.wait("vector", "xb", 16 * (dc // 4 + 1))
                if dc == 0:
                    S.wait("tensor", "xb", 16)
            if dc >= 2:
                S.wait("vector", "pe", sq_pe[dc - 2])
            S.emit("vector", lambda e, dc=dc: e.tensor_mul(
                a_ysq[:, (dc % 2) * T:((dc % 2) + 1) * T],
                src_t[:, dc * T:(dc + 1) * T], src_t[:, dc * T:(dc + 1) * T]),
                inc="v")
            sq_v = S.emit("vector", lambda e, dc=dc: e.tensor_copy(
                s_fl[:, 0:8],
                a_ysq[:, (dc % 2) * T + T - 8:((dc % 2) + 1) * T]), inc="v")
            S.wait("tensor", "v", sq_v)
            if dc == 0 and first_war is not None:
                S.wait("tensor", first_war[0], first_war[1])
            if xb_gate and dc % 4 == 0 and dc > 0:
                S.wait("tensor", "xb", 16 * (dc // 4 + 1))
            for half in range(2):
                S.emit("tensor", lambda e, dc=dc, h=half: e.matmul(
                    bankrow(h, 0, 512), s_ones[:, :],
                    src_t[:, dc * T + h * 512: dc * T + (h + 1) * 512],
                    start=(dc == 0), stop=(dc == DCN - 1)))
            for half in range(2):
                sq_pe[dc] = S.emit("tensor", lambda e, dc=dc, h=half: e.matmul(
                    bankrow(2 + h, 0, 512), s_ones[:, :],
                    a_ysq[:, (dc % 2) * T + h * 512:((dc % 2) * T + (h + 1) * 512)],
                    start=(dc == 0), stop=(dc == DCN - 1)), inc="pe")
        return sq_pe

    XN1_V = None
    if nphases >= 1:
        S.wait("vector", "c", NCONST)
        S.wait("tensor", "c", NCONST)
        S.wait("scalar", "c", NCONST)
        sq_pe = ln_stats(s_xb, xb_gate=True)
        S.wait("vector", "pe", sq_pe[DCN - 1])
        bc_t = ln_finalize_bcast(0, 1024, 2048)
        S.wait("vector", "pe", bc_t)
        S.emit("vector", lambda e: e.tensor_copy(a_scr[0:1, 0:1024], ps[0:1, 3072:4096]))
        XN1_DC = {}
        for dc in range(DCN):
            S.emit("vector", lambda e, dc=dc: e.tensor_sub(
                a_scr[:, :], s_xb[:, dc * T:(dc + 1) * T], ps[:, 2048:3072]))
            XN1_V = S.emit("vector", lambda e, dc=dc: e.tensor_mul(
                s_xn[:, dc * T:(dc + 1) * T], a_scr[:, :], ps[:, 3072:4096]),
                inc="v")
            XN1_DC[dc] = XN1_V

    def emit_rope(buf, col_off):
        A = buf[:, col_off:col_off + T]
        Alo = buf[0:64, col_off:col_off + T]
        Ahi = buf[64:128, col_off:col_off + T]
        def f(e):
            e.tensor_copy(a_r1[0:64, :], Ahi)
            e.tensor_copy(a_r1[64:128, :], Alo)
            e.tensor_mul(a_r2[:, :], A, a_cs[:, :])
            e.tensor_mul(a_r1[:, :], a_r1[:, :], a_sn[:, :])
            return e.tensor_add(A, a_r2[:, :], a_r1[:, :])
        return S.emit("vector", f, inc="v")

    def proj_ws(n_oc, wsrc_ap, out_cb, first_war=None, after_wload=None,
                npairs=2):
        evac_t = {}
        for oc in range(n_oc):
            slot, wsem, wtgt = wload(
                wsrc_ap, [[128, 128], [128 * 128, DCN], [1, 128]],
                oc * (DCN * 128 * 128))
            if after_wload is not None and oc == 1:
                after_wload()
            pair = (oc % npairs) * 2
            S.wait("tensor", wsem, wtgt)
            if oc == 0 and first_war is not None:
                S.wait("tensor", first_war[0], first_war[1])
            if oc == 1:
                S.wait("tensor", "v", XN1_V)
            if oc >= npairs:
                S.wait("tensor", "a", evac_t[oc - npairs])
            for half in range(2):
                for dc in range(DCN):
                    last = (half == 1) and (dc == DCN - 1)
                    if oc == 0 and half == 0:
                        S.wait("tensor", "v", XN1_DC[dc])
                    S.emit("tensor", lambda e, s=slot, dc=dc, h=half, p=pair: e.matmul(
                        bank(p + h),
                        s_wb[:, s + dc * 128: s + (dc + 1) * 128],
                        s_xn[:, dc * T + h * 512: dc * T + (h + 1) * 512],
                        start=(dc == 0), stop=(dc == DCN - 1)),
                        inc="pw" if last else None)
            S.wait("scalar", "pw", S.cnt["pw"])
            evac_t[oc] = out_cb(oc, pair)
        return evac_t

    if nphases >= 2:
        wfamily(4)
        def k_out(oc, pair):
            a_t = S.emit("scalar", lambda e, oc=oc, p=pair: e.activation(
                s_kot[:, oc * T:(oc + 1) * T], ps[:, p * 512:(p + 2) * 512], AF.Copy),
                inc="a")
            S.wait("vector", "a", a_t)
            emit_rope(s_kot, oc * T)
            return a_t
        proj_ws(H, wk_b, k_out)
        KROPE_V = S.cnt["v"]
        S.wait("sync", "v", KROPE_V)
        S.dma("sync",
              dap(kv_loc.ap(), 0, [[T, 128], [128 * T, H], [1, T]]),
              s_kot.ap().rearrange("p (hc t) -> p hc t", hc=H),
              "kst")

    if nphases >= 3:
        wfamily(4)
        vev_t = {}
        vst_t = {}
        for dg in range(4):
            vslots = []
            for qt in range(4):
                sl, sm, tg = wload(
                    wv_b, [[512, 128], [128 * 512, 4], [1, 512]],
                    (dg * DCN + qt * 4) * (128 * 512))
                vslots.append((sl, sm, tg))
            for tc in range(8):
                key = dg * 8 + tc
                vpair = 4 + (tc % 2)
                if key >= 2:
                    S.wait("tensor", "a", vev_t[key - 2])
                pe_done = None
                for dc in range(DCN):
                    if tc == 0 and dc % 4 == 0:
                        S.wait("tensor", vslots[dc // 4][1], vslots[dc // 4][2])
                    slot = vslots[dc // 4][0]
                    i4 = dc % 4
                    mmfn = (lambda e, s=slot, i4=i4, dc=dc, tc=tc, p=vpair:
                            e.matmul(
                                bank(p),
                                s_xn[:, dc * T + tc * 128: dc * T + (tc + 1) * 128],
                                s_wb[:, s + i4 * 512:(s + (i4 + 1) * 512)],
                                start=(dc == 0), stop=(dc == DCN - 1)))
                    pw_inc = (tc == 7) and (dc % 4 == 3)
                    if dc == DCN - 1:
                        incs = ["pe"] + (["pw"] if pw_inc else [])
                        pe_done = S.emit_m("tensor", mmfn, incs)["pe"]
                    else:
                        S.emit("tensor", mmfn, inc="pw" if pw_inc else None)
                S.wait("scalar", "pe", pe_done)
                vev_t[key] = S.emit("scalar", lambda e, tc=tc, dg=dg, p=vpair: e.activation(
                    a_vs[:, tc * D + dg * 512: tc * D + (dg + 1) * 512],
                    bank(p), AF.Copy), inc="a")
                if dg == 3:
                    S.wait("sync", "a", vev_t[key])
                    kvflat = kv_loc.ap().rearrange("a b -> (a b)")
                    vst_t[tc] = S.dma(
                        "sync",
                        dap(kvflat, D * T + tc * 16384,
                            [[128, 128], [T * 128, H], [1, 128]]),
                        a_vs[:, tc * D:(tc + 1) * D].rearrange(
                            "p (h d) -> p h d", h=H),
                        "kst")
        KST_ALL = S.cnt["kst"]

    if nphases >= 4:
        a_before_q = S.cnt["a"]
        trig = {"done": False}

        def maybe_trigger():
            if trig["done"]:
                return
            trig["done"] = True
            S.wait("gpsimd", "kst", KST_ALL)
            S.emit("gpsimd", lambda e: e.collective_compute(
                "AllGather", OP.bypass, replica_groups=[list(range(NCORES))],
                ins=[kv_loc.ap()], outs=[kv_all.ap()]), inc="cck")

        def q_out(oc, pair):
            if oc >= 2:
                S.wait("scalar", "qst%d" % (oc % 2), 16 * (oc // 2))
            a_t = S.emit("scalar", lambda e, oc=oc, p=pair: e.activation(
                a_qh[:, (oc % 2) * T:((oc % 2) + 1) * T],
                ps[:, p * 512:(p + 2) * 512], AF.Copy), inc="a")
            S.wait("vector", "a", a_t)
            r_t = emit_rope(a_qh, (oc % 2) * T)
            S.wait("sync", "v", r_t)
            S.dma("sync", q_spill[oc * 128:(oc + 1) * 128, :],
                  a_qh[:, (oc % 2) * T:((oc % 2) + 1) * T], "qst%d" % (oc % 2))
            return a_t
        wfamily(4)
        proj_ws(H, wq_b, q_out, first_war=("a", a_before_q),
                after_wload=maybe_trigger, npairs=3)

    VC_ALL = None
    if nphases >= 5:
        A_Q_END = S.cnt["a"]
        V_Q_END = S.cnt["v"]
        ONES_V = S.emit("vector", lambda e, a=ones_ap: e.memset(a, 1.0), inc="v")
        exp_tgt, pv_tgt, an_tgt, tr_tgt, vc_tgt = {}, {}, {}, {}, {}
        vc_half = {}
        kh_tgt, vh_tgt, qld_tgt = {}, {}, {}
        upv_last = {}
        kvallk = kv_all.ap()
        kvall_flat = kvallk.rearrange("a b -> (a b)")

        QON = 2 * T

        def o_cols(qs):
            return (6 * 512 + qs * 130) if qs < 3 else (7 * 512)

        batches = []
        g = 0
        for u in range(32):
            cbase = 0
            for bi, w in enumerate(BATCHES):
                batches.append((u, bi, w, cbase, g))
                cbase += w
                g += 1
        NG = g
        ginfo = {bg[4]: bg for bg in batches}

        def emit_pv(g2):
            u2, bi2, w2, cbase2, _ = ginfo[g2]
            h2 = u2 // 2
            grp2 = g2 % PSLOT
            S.wait("tensor", "p", exp_tgt[g2])
            if bi2 == 0:
                if u2 == 0:
                    S.wait("tensor", "v", ONES_V)
                if u2 % 2 == 0:
                    S.wait("tensor", "vh%d" % (h2 % 2), vh_tgt[h2])
            for i in range(w2):
                j = cbase2 + i
                for qs in range(4):
                    if bi2 == 0 and i == 0 and u2 >= 1:
                        if qs == 0:
                            S.wait("tensor", "vc", vc_half[u2 - 1])
                        elif qs == 3:
                            S.wait("tensor", "vc", vc_tgt[u2 - 1])
                    last = (i == w2 - 1) and (qs == 3)
                    st = (j == 0) and (qs in (0, 3))
                    sp = (j == KC - 1) and (qs in (2, 3))
                    S.emit("tensor",
                           lambda e, h=h2, pt=grp2, i=i, j=j, qs=qs, st=st, sp=sp:
                           e.matmul(
                               ps[:, o_cols(qs): o_cols(qs) + 129],
                               a_pt[:, pt * 1024 + i * 512 + qs * 128:
                                    (pt * 1024 + i * 512 + (qs + 1) * 128)],
                               a_vh[:, (h % 2) * 8256 + j * 129:
                                    ((h % 2) * 8256 + (j + 1) * 129)],
                               start=st, stop=sp, skip_group_check=True),
                           inc="pv" if last else None)
            pv_tgt[g2] = S.cnt["pv"]
            if bi2 == NB - 1:
                upv_last[u2] = pv_tgt[g2]

        ev_bounce = {}

        def emit_evac1(uev):
            S.wait("scalar", "pv", upv_last[uev])
            if uev >= 1:
                S.wait("scalar", "tr", tr_tgt[uev - 1])
            def evac1(e):
                for qs in range(4):
                    e.activation(a_rs[:, qs:qs + 1],
                                 ps[:, o_cols(qs) + 128: o_cols(qs) + 129], AF.Ln)
                e.activation(a_rs[:, 4:8], a_rs[:, 0:4], AF.Exp, scale=-1.0)
                e.activation(a_rs[:, 8:12], a_rs[:, 4:8], AF.Copy)
                return e.activation(a_rs[:, 0:4], a_rs[:, 8:12], AF.Copy)
            ev_bounce[uev] = S.emit("scalar", evac1, inc="a")

        def emit_evac2(uev):
            S.wait("vector", "a", ev_bounce[uev])
            for qs in range(4):
                an_tgt[uev] = S.emit("vector", lambda e, qs=qs: e.tensor_scalar_mul(
                    s_xn[:, QON + qs * 128: QON + (qs + 1) * 128],
                    ps[:, o_cols(qs): o_cols(qs) + 128],
                    a_rs[:, 8 + qs:9 + qs]), inc="an")

        TRT = {0: 3072, 1: 3200, 2: 3584, 3: 3712}

        def emit_trcopy(pu):
            S.wait("tensor", "an", an_tgt[pu])
            if pu >= 1:
                S.wait("tensor", "vc", vc_tgt[pu - 1])
            tr_half = None
            for qs in range(4):
                t = S.emit("tensor", lambda e, qs=qs: e.matmul(
                    ps[:, TRT[qs]: TRT[qs] + 128],
                    s_xn[:, QON + qs * 128: QON + (qs + 1) * 128], s_id[:, :],
                    start=(qs in (0, 2)), stop=True, skip_group_check=True),
                    inc="tr" if qs in (1, 3) else None)
                if qs == 1:
                    tr_half = t
                elif qs == 3:
                    tr_tgt[pu] = t
            ph_, pq = pu // 2, pu % 2
            S.wait("vector", "tr", tr_half)
            for qs in range(4):
                if qs == 2:
                    S.wait("vector", "tr", tr_tgt[pu])
                t = S.emit("vector", lambda e, ph=ph_, pq=pq, qs=qs: e.tensor_copy(
                    s_kot[:, ph * T + pq * 512 + qs * 128:
                          (ph * T + pq * 512 + (qs + 1) * 128)],
                    ps[:, TRT[qs]: TRT[qs] + 128]),
                    inc="vc" if qs in (1, 3) else None)
                if qs == 1:
                    vc_half[pu] = t
                elif qs == 3:
                    vc_tgt[pu] = t

        for (u, bi, w, cbase, g) in batches:
            h, qh = u // 2, u % 2
            if bi == 0 and qh == 0:
                b = h % 2
                if h >= 2:
                    S.wait("sync", "pv", upv_last[2 * h - 3])
                S.wait("sync", "qst%d" % b, 16 * (h // 2 + 1))
                qld_tgt[h] = S.dma(
                    "sync", s_xn[:, b * T:(b + 1) * T],
                    q_spill[h * 128:(h + 1) * 128, :], "qld%d" % b)
                if h < 2:
                    S.wait("sync", "cck", 1)
                kh_tgt[h] = S.dma(
                    "sync",
                    a_kh[:, b * 8192:(b + 1) * 8192].rearrange("p (r t) -> p r t", r=8),
                    dap(kvallk, h * 128 * T,
                        [[T, 128], [4 * T * T, NCORES], [1, T]]),
                    "kh%d" % b)
                if h < 2:
                    S.wait("sync", "v", ONES_V)
                for r in range(NCORES):
                    vh_tgt[h] = S.dma(
                        "sync",
                        dap(a_vh.ap(), b * 8256 + r * 1032,
                            [[2 * 8256, 128], [129, 8], [1, 128]]),
                        dap(kvall_flat, r * 4 * T * T + D * T + h * T * 128,
                            [[128, 128], [16384, 8], [1, 128]]),
                        "vh%d" % b)

            grp = g % 3
            ptile = g % PSLOT
            if g == 0:
                S.wait("tensor", "a", A_Q_END)
            if g >= 3:
                S.wait("tensor", "p", exp_tgt[g - 3])
            if bi == 0 and qh == 0:
                S.wait("tensor", "kh%d" % (h % 2), kh_tgt[h])
                S.wait("tensor", "qld%d" % (h % 2), qld_tgt[h])
            for i in range(w):
                j = cbase + i
                S.emit("tensor", lambda e, h=h, qh=qh, j=j, grp=grp, i=i: e.matmul(
                    ps[:, grp * 1024 + i * 512: grp * 1024 + (i + 1) * 512],
                    a_kh[:, (h % 2) * 8192 + j * 128:((h % 2) * 8192 + (j + 1) * 128)],
                    s_xn[:, (h % 2) * T + qh * 512:((h % 2) * T + (qh + 1) * 512)],
                    start=True, stop=True),
                    inc="s" if i == w - 1 else None)
            s_tgt_g = S.cnt["s"]
            if g - PVLAG >= 0:
                u3e, bi3e = ginfo[g - PVLAG][0], ginfo[g - PVLAG][1]
                if bi3e == 0 and u3e >= 1:
                    emit_evac2(u3e - 1)
            if bi == TRC_BI and u >= 1:
                emit_trcopy(u - 1)
            if PVLAG <= bi < TRC_BI and u >= 1:
                pass
            elif bi == TRC_BI and u >= 1:
                for x in range(g - TRC_BI, g - PVLAG + 1):
                    emit_pv(x)
            elif g >= PVLAG:
                emit_pv(g - PVLAG)
            if g == 0:
                S.wait("scalar", "v", V_Q_END)
            S.wait("scalar", "s", s_tgt_g)
            if g >= PSLOT:
                S.wait("scalar", "pv", pv_tgt[g - PSLOT])
            exp_tgt[g] = S.emit("scalar", lambda e, grp=grp, pt=ptile, w=w: e.activation(
                a_pt[:, pt * 1024: pt * 1024 + w * 512],
                ps[:, grp * 1024: grp * 1024 + w * 512],
                AF.Exp, bias=s_cb[:, 0:1], scale=EXP_SCALE), inc="p")
            uev = None
            if g - PVLAG >= 0:
                u3, bi3 = ginfo[g - PVLAG][0], ginfo[g - PVLAG][1]
                if bi3 == NB - 1:
                    uev = u3
            if uev is not None:
                emit_evac1(uev)

        for g2 in range(NG - PVLAG, NG):
            emit_pv(g2)
        emit_evac1(31)
        emit_evac2(31)
        emit_trcopy(31)
        VC_ALL = S.cnt["vc"]

    if nphases >= 6:
        wfamily(4)
        S.wait("sync", "pv", S.cnt["pv"])
        X2LD = S.dma("sync",
                     a_x2.ap().rearrange("p (dc t) -> p dc t", dc=DCN),
                     dap(xT, 0, [[T, 128], [128 * T, DCN], [1, T]]),
                     "xc0")
        ev_t = {}
        ostat_pe_t = {}
        for ec in range(ECN):
            slot, wsem, wtgt = wload(
                wo_b, [[128, 128], [128 * 128, DCN], [1, 128]],
                ec * (DCN * 128 * 128))
            pair = (ec % 2) * 2
            S.wait("tensor", wsem, wtgt)
            if ec == 0:
                S.wait("tensor", "vc", VC_ALL)
            if ec >= 2:
                S.wait("tensor", "v", ev_t[ec - 2])
            for half in range(2):
                for dc in range(DCN):
                    last = half == 1 and dc == DCN - 1
                    S.emit("tensor", lambda e, s=slot, dc=dc, h=half, p=pair: e.matmul(
                        bank(p + h),
                        s_wb[:, s + dc * 128: s + (dc + 1) * 128],
                        s_kot[:, dc * T + h * 512: dc * T + (h + 1) * 512],
                        start=(dc == 0), stop=(dc == DCN - 1)),
                        inc="pw" if last else None)
            S.wait("vector", "pw", S.cnt["pw"])
            if ec >= 2:
                S.wait("vector", "pe", ostat_pe_t[ec - 2])
            S.emit("vector", lambda e, ec=ec, p=pair: e.tensor_copy(
                s_xb[:, ec * T:(ec + 1) * T], ps[:, p * 512:(p + 2) * 512]), inc="v")
            S.emit("vector", lambda e, ec=ec: e.tensor_mul(
                a_ysq[:, (ec % 2) * T:((ec % 2) + 1) * T],
                s_xb[:, ec * T:(ec + 1) * T], s_xb[:, ec * T:(ec + 1) * T]), inc="v")
            sq_v = S.emit("vector", lambda e, ec=ec: e.tensor_copy(
                s_fl[:, 0:8],
                a_ysq[:, (ec % 2) * T + T - 8:((ec % 2) + 1) * T]), inc="v")
            ev_t[ec] = sq_v
            S.wait("tensor", "v", sq_v)
            for half in range(2):
                S.emit("tensor", lambda e, ec=ec, h=half: e.matmul(
                    bankrow(4 + h, 0, 512), s_ones[:, :],
                    s_xb[:, ec * T + h * 512: ec * T + (h + 1) * 512],
                    start=(ec == 0), stop=(ec == ECN - 1)))
            for half in range(2):
                ostat_pe_t[ec] = S.emit("tensor", lambda e, ec=ec, h=half: e.matmul(
                    bankrow(6 + h, 0, 512), s_ones[:, :],
                    a_ysq[:, (ec % 2) * T + h * 512:((ec % 2) * T + (h + 1) * 512)],
                    start=(ec == 0), stop=(ec == ECN - 1)),
                    inc="pe" if half == 1 else None)
        OSTAT_PE = S.cnt["pe"]
        S.wait("vector", "pe", OSTAT_PE)
        bc_t = ln_finalize_bcast(2048, 3072, 0, dbg_rows=(dbg == "stats"))
        if dbg == "stats":
            v_t2 = S.emit("vector", lambda e: e.tensor_copy(
                a_scr[0:1, 0:1024], s_rows[0:1, 1024:2048]), inc="v")
            S.wait("sync", "v", v_t2)
            S.dma("sync", dbgst[2:3, :], a_scr[0:1, 0:1024], "dbg")
        mbrb_copy(bc_t)
        x2v_t = {}
        ln3_pe = {}
        for dc in range(DCN):
            S.wait("vector", "xc0", X2LD)
            def f2(e, dc=dc):
                e.tensor_sub(a_scr[:, :], s_xb[:, dc * T:(dc + 1) * T], a_mb[:, :])
                e.tensor_mul(a_scr[:, :], a_scr[:, :], a_rb[:, :])
                return e.tensor_add(s_xb[:, dc * T:(dc + 1) * T],
                                    a_x2[:, dc * T:(dc + 1) * T], a_scr[:, :])
            x2v_t[dc] = S.emit("vector", f2, inc="v")
            S.emit("vector", lambda e, dc=dc: e.tensor_copy(
                s_fl[:, 0:8], s_xb[:, (dc + 1) * T - 8:(dc + 1) * T]), inc="v")
            if dc >= 2:
                S.wait("vector", "pe", ln3_pe[dc - 2])
            S.emit("vector", lambda e, dc=dc: e.tensor_mul(
                a_ysq[:, (dc % 2) * T:((dc % 2) + 1) * T],
                s_xb[:, dc * T:(dc + 1) * T], s_xb[:, dc * T:(dc + 1) * T]),
                inc="v")
            sq3_v = S.emit("vector", lambda e, dc=dc: e.tensor_copy(
                s_fl[:, 0:8],
                a_ysq[:, (dc % 2) * T + T - 8:((dc % 2) + 1) * T]), inc="v")
            S.wait("tensor", "v", sq3_v)
            for half in range(2):
                S.emit("tensor", lambda e, dc=dc, h=half: e.matmul(
                    bankrow(h, 0, 512), s_ones[:, :],
                    s_xb[:, dc * T + h * 512: dc * T + (h + 1) * 512],
                    start=(dc == 0), stop=(dc == DCN - 1)))
            for half in range(2):
                ln3_pe[dc] = S.emit("tensor", lambda e, dc=dc, h=half: e.matmul(
                    bankrow(2 + h, 0, 512), s_ones[:, :],
                    a_ysq[:, (dc % 2) * T + h * 512:((dc % 2) * T + (h + 1) * 512)],
                    start=(dc == 0), stop=(dc == DCN - 1)), inc="pe")

    XN3_V = None
    if nphases >= 7:
        S.wait("vector", "pe", ln3_pe[DCN - 1])
        bc_t = ln_finalize_bcast(0, 1024, 2048)
        mbrb_copy(bc_t)
        XN3_DC = {}
        for dc in range(DCN):
            S.emit("vector", lambda e, dc=dc: e.tensor_sub(
                a_ysq[:, 0:T], s_xb[:, dc * T:(dc + 1) * T], a_mb[:, :]))
            XN3_V = S.emit("vector", lambda e, dc=dc: e.tensor_mul(
                s_xn[:, dc * T:(dc + 1) * T], a_ysq[:, 0:T], a_rb[:, :]),
                inc="v")
            XN3_DC[dc] = XN3_V
        S.wait("sync", "v", XN3_V)
        S.dma("sync",
              dap(x2_spill.ap(), 0, [[T, 128], [128 * T, DCN], [1, T]]),
              s_xb.ap().rearrange("p (dc t) -> p dc t", dc=DCN), "sp")

    dstat_v = {}
    if nphases >= 8:
        fuse_v = {}
        for th in range(2):
            wfamily(4)
            for ic in range(ICN):
                gslot, gsem, gtgt = wload(
                    wg_b, [[128, 128], [128 * 128, DCN], [1, 128]],
                    ic * (DCN * 128 * 128))
                uslot, usem, utgt = wload(
                    wu_b, [[128, 128], [128 * 128, DCN], [1, 128]],
                    ic * (DCN * 128 * 128))
                gb = (ic % 4) * 2
                S.wait("tensor", gsem, gtgt)
                if th == 0 and ic == 1:
                    S.wait("tensor", "v", XN3_V)
                if th == 1 and ic == 0:
                    S.wait("tensor", "v", dstat_v[0])
                if ic >= 4:
                    S.wait("tensor", "v", fuse_v[(th, ic - 4)])
                for dc in range(DCN):
                    if th == 0 and ic == 0:
                        S.wait("tensor", "v", XN3_DC[dc])
                    S.emit("tensor", lambda e, s=gslot, dc=dc, th=th, b=gb: e.matmul(
                        bank(b),
                        s_wb[:, s + dc * 128: s + (dc + 1) * 128],
                        s_xn[:, dc * T + th * 512: dc * T + (th + 1) * 512],
                        start=(dc == 0), stop=(dc == DCN - 1)),
                        inc="pw" if dc == DCN - 1 else None)
                S.wait("tensor", usem, utgt)
                for dc in range(DCN):
                    S.emit("tensor", lambda e, s=uslot, dc=dc, th=th, b=gb: e.matmul(
                        bank(b + 1),
                        s_wb[:, s + dc * 128: s + (dc + 1) * 128],
                        s_xn[:, dc * T + th * 512: dc * T + (th + 1) * 512],
                        start=(dc == 0), stop=(dc == DCN - 1)),
                        inc="pw" if dc == DCN - 1 else None)
                pw_now = S.cnt["pw"]
                S.wait("scalar", "pw", pw_now)
                if ic >= 2:
                    S.wait("scalar", "v", fuse_v[(th, ic - 2)])
                S.emit("scalar", lambda e, ic=ic, b=gb: e.activation(
                    a_gu[:, (ic % 2) * 512:((ic % 2) + 1) * 512],
                    bank(b), AF.Silu, scale=s_sg[:, ic:ic + 1]), inc="a")
                silu_t = S.emit("scalar", lambda e, ic=ic: e.activation(
                    s_fl[:, 0:8],
                    a_gu[:, (ic % 2) * 512 + 504:((ic % 2) + 1) * 512], AF.Copy),
                    inc="a")
                S.wait("vector", "a", silu_t)
                S.emit("vector", lambda e, ic=ic, b=gb: e.tensor_scalar_mul(
                    a_gu[:, 1024 + (ic % 2) * 512: 1024 + ((ic % 2) + 1) * 512],
                    bank(b + 1), s_su[:, ic:ic + 1]))
                if th == 0 and ic == 0:
                    S.wait("vector", "sp", 16)
                S.emit("vector", lambda e, ic=ic: e.tensor_mul(
                    a_f[:, ic * 512:(ic + 1) * 512],
                    a_gu[:, (ic % 2) * 512:((ic % 2) + 1) * 512],
                    a_gu[:, 1024 + (ic % 2) * 512: 1024 + ((ic % 2) + 1) * 512]),
                    inc="v")
                fuse_v[(th, ic)] = S.emit("vector", lambda e, ic=ic: e.tensor_copy(
                    s_fl[:, 0:8], a_f[:, ic * 512 + 504:(ic + 1) * 512]), inc="v")
            wfamily(2)
            ev_v = {}
            for ec in range(ECN):
                yb = 4 + (ec % 2)
                if ec == 0:
                    S.wait("tensor", "v", fuse_v[(th, ICN - 1)])
                if ec >= 2:
                    S.wait("tensor", "v", ev_v[ec - 2])
                for halfw in range(2):
                    slot, wsem, wtgt = wload(
                        wd_b, [[128, 128], [128 * 128, 32], [1, 128]],
                        (ec * ICN + halfw * 32) * (128 * 128))
                    S.wait("tensor", wsem, wtgt)
                    for i32 in range(32):
                        j = halfw * 32 + i32
                        S.emit("tensor", lambda e, s=slot, i32=i32, j=j, b=yb: e.matmul(
                            bank(b),
                            s_wb[:, s + i32 * 128: s + (i32 + 1) * 128],
                            a_f[:, j * 512:(j + 1) * 512],
                            start=(j == 0), stop=(j == ICN - 1)),
                            inc="pw" if i32 == 31 else None)
                pw_now = S.cnt["pw"]
                S.wait("vector", "pw", pw_now)
                S.emit("vector", lambda e, ec=ec, b=yb, th=th: e.tensor_scalar_mul(
                    s_xb[:, ec * T + th * 512: ec * T + (th + 1) * 512],
                    bank(b), s_sd[:, ec:ec + 1]), inc="v")
                S.emit("vector", lambda e, ec=ec, th=th: e.tensor_mul(
                    a_ysq[:, (ec % 2) * 512:((ec % 2) + 1) * 512],
                    s_xb[:, ec * T + th * 512: ec * T + (th + 1) * 512],
                    s_xb[:, ec * T + th * 512: ec * T + (th + 1) * 512]), inc="v")
                ev_v[ec] = S.emit("vector", lambda e, ec=ec: e.tensor_copy(
                    s_fl[:, 0:8],
                    a_ysq[:, (ec % 2) * 512 + 504:((ec % 2) + 1) * 512]), inc="v")
                S.wait("tensor", "v", ev_v[ec])
                S.emit("tensor", lambda e, ec=ec, th=th: e.matmul(
                    bankrow(0, 0, 512), s_ones[:, :],
                    s_xb[:, ec * T + th * 512: ec * T + (th + 1) * 512],
                    start=(ec == 0), stop=(ec == ECN - 1)))
                S.emit("tensor", lambda e, ec=ec: e.matmul(
                    bankrow(1, 0, 512), s_ones[:, :],
                    a_ysq[:, (ec % 2) * 512:((ec % 2) + 1) * 512],
                    start=(ec == 0), stop=(ec == ECN - 1)),
                    inc="pe" if ec == ECN - 1 else None)
            sp_t = pe_spacer(1024)
            S.wait("scalar", "pe", sp_t)
            S.emit("scalar", lambda e, th=th: e.activation(
                s_rows[0:1, th * 512:(th + 1) * 512],
                bankrow(0, 0, 512), AF.Copy), inc="a")
            a_t = S.emit("scalar", lambda e, th=th: e.activation(
                s_rows[0:1, 1024 + th * 512: 1024 + (th + 1) * 512],
                bankrow(1, 0, 512), AF.Copy), inc="a")
            S.wait("vector", "a", a_t)
            dstat_v[th] = S.emit("vector", lambda e: e.engine_nop(), inc="v")
            if th == 1:
                DOWN_END_PE = S.cnt["pe"]

    if nphases >= 9:
        S.wait("sync", "v", dstat_v[1])
        la0 = launder_rows(0, 2048, ("a", S.cnt["a"]))
        S.wait("vector", "la", la0)
        def f3(e):
            e.tensor_scalar_mul(s_rows[0:1, 0:1024], s_rows[0:1, 0:1024], 1.0 / 2048.0)
            e.tensor_scalar_mul(s_rows[0:1, 1024:2048], s_rows[0:1, 1024:2048],
                                1.0 / 2048.0)
            e.tensor_mul(s_rows[0:1, 2048:3072], s_rows[0:1, 0:1024],
                         s_rows[0:1, 0:1024])
            return e.tensor_sub(s_rows[0:1, 1024:2048], s_rows[0:1, 1024:2048],
                                s_rows[0:1, 2048:3072])
        v_t = S.emit("vector", f3, inc="v")
        la1 = launder_rows(0, 2048, ("v", v_t))
        S.wait("scalar", "la", la1)
        def g3(e):
            e.activation(s_rows[0:1, 2048:3072], s_rows[0:1, 1024:2048], AF.Ln,
                         bias=s_cb[0:1, 1:2])
            return e.activation(s_rows[0:1, 2048:3072], s_rows[0:1, 2048:3072],
                                AF.Exp, scale=-0.5)
        a_t = S.emit("scalar", g3, inc="a")
        la2 = launder_rows(2048, 3072, ("a", a_t))
        S.wait("tensor", "la", la2)
        for half in range(2):
            S.emit("tensor", lambda e, h=half: e.matmul(
                bank(4 + h), s_or[:, :], s_rows[0:1, h * 512:(h + 1) * 512],
                start=True, stop=True))
        for half in range(2):
            bc_t = S.emit("tensor", lambda e, h=half: e.matmul(
                bank(6 + h), s_or[:, :],
                s_rows[0:1, 2048 + h * 512: 2048 + (h + 1) * 512],
                start=True, stop=True), inc="pe" if half == 1 else None)
        mbrb_copy(bc_t)
        out_v = {}
        S.wait("scalar", "pe", DOWN_END_PE)
        for dc in range(DCN):
            b = dc % 4
            if dc >= 4:
                S.wait("scalar", "v", out_v[dc - 4])
            x2l_t = S.dma("scalar", a_xc4[:, b * T:(b + 1) * T],
                          x2_spill[dc * 128:(dc + 1) * 128, :], "x2l%d" % b)
            S.wait("vector", "x2l%d" % b, x2l_t)
            if dc >= 4:
                S.wait("vector", "yst%d" % b, 16 * (dc // 4))
            def f4(e, dc=dc, b=b):
                e.tensor_sub(a_ysq[:, 0:T], s_xb[:, dc * T:(dc + 1) * T], a_mb[:, :])
                e.tensor_mul(a_ysq[:, 0:T], a_ysq[:, 0:T], a_rb[:, :])
                return e.tensor_add(a_out4[:, b * T:(b + 1) * T],
                                    a_xc4[:, b * T:(b + 1) * T], a_ysq[:, 0:T])
            S.emit("vector", f4, inc="v")
            out_v[dc] = S.emit("vector", lambda e, b=b: e.tensor_copy(
                s_fl[:, 0:2], a_out4[:, (b + 1) * T - 2:(b + 1) * T]), inc="v")
            S.wait("sync", "v", out_v[dc])
            S.dma("sync", y_out_h[dc * 128:(dc + 1) * 128, :],
                  a_out4[:, b * T:(b + 1) * T], "yst%d" % b)

    dbg_src = None
    if dbg == "xn":
        dbg_t = nc.dram_tensor("dbg", [128, DCN * T], BF16, kind="ExternalOutput").ap()
        dbg_src = s_xn[:, :]
    elif dbg == "kot":
        dbg_t = nc.dram_tensor("dbg", [128, H * T], BF16, kind="ExternalOutput").ap()
        dbg_src = s_kot[:, :]
    elif dbg == "xb":
        dbg_t = nc.dram_tensor("dbg", [128, DCN * T], BF16, kind="ExternalOutput").ap()
        dbg_src = s_xb[:, :]
    elif dbg == "x2":
        dbg_t = nc.dram_tensor("dbg", [128, DCN * T], BF16, kind="ExternalOutput").ap()
        dbg_src = s_xb[:, :]
    elif dbg == "kvall":
        dbg_t = nc.dram_tensor("dbg", [NCORES * 4 * T, T], BF16,
                               kind="ExternalOutput").ap()
        dbg_src = kv_all.ap()
    elif dbg == "qspill":
        dbg_t = nc.dram_tensor("dbg", [D, T], BF16, kind="ExternalOutput").ap()
        dbg_src = q_spill.ap()


    for name in sem_names:
        if name == "dbg" or S.cnt[name] == 0:
            continue
        S.wait("sync", name, S.cnt[name])
    if dbg_src is not None:
        S.dma("sync", dbg_t, dbg_src, "dbg")
        S.wait("sync", "dbg", 16)

    with nc.Block() as block:
        @block.tensor
        def _(e):
            S.run("tensor", e)

        @block.vector
        def _(e):
            S.run("vector", e)

        @block.scalar
        def _(e):
            S.run("scalar", e)

        @block.sync
        def _(e):
            S.run("sync", e)

        @block.gpsimd
        def _(e):
            S.run("gpsimd", e)

    return nc



def _prep_inputs(L0_attn_x, L0_attn_wq, L0_attn_wk, L0_attn_wv, L0_attn_wo,
                 L0_ffn_w_gate, L0_ffn_w_up, L0_ffn_w_down,
                 L0_ffn_s_gate, L0_ffn_s_up, L0_ffn_s_down):
    x = np.asarray(L0_attn_x)
    wq = np.asarray(L0_attn_wq, dtype=np.float32)
    wk = np.asarray(L0_attn_wk, dtype=np.float32)
    wv = np.asarray(L0_attn_wv, dtype=np.float32)
    wo = np.asarray(L0_attn_wo, dtype=np.float32)
    wg = np.asarray(L0_ffn_w_gate)
    wu = np.asarray(L0_ffn_w_up)
    wd = np.asarray(L0_ffn_w_down)

    deint = np.concatenate([np.arange(0, 128, 2), np.arange(1, 128, 2)])
    perm = np.concatenate([h * 128 + deint for h in range(H)])

    def blk_oc(w_lay):
        return np.ascontiguousarray(
            w_lay.reshape(DCN, 128, w_lay.shape[1] // 128, 128).transpose(2, 0, 1, 3))

    wq_b = blk_oc(wq.T[:, perm])
    wk_b = blk_oc(wk.T[:, perm])
    wo_b = blk_oc(wo.T)
    wv_b = np.ascontiguousarray(
        wv.T.reshape(DCN, 128, 4, 512).transpose(2, 0, 1, 3))
    wg_b = np.ascontiguousarray(wg.reshape(DCN, 128, ICN, 128).transpose(2, 0, 1, 3))
    wu_b = np.ascontiguousarray(wu.reshape(DCN, 128, ICN, 128).transpose(2, 0, 1, 3))
    wd_b = np.ascontiguousarray(wd.reshape(ICN, 128, ECN, 128).transpose(2, 0, 1, 3))

    sg = np.ascontiguousarray(np.asarray(L0_ffn_s_gate, np.float32).reshape(ICN, 128).T)
    su = np.ascontiguousarray(np.asarray(L0_ffn_s_up, np.float32).reshape(ICN, 128).T)
    sd = np.ascontiguousarray(np.asarray(L0_ffn_s_down, np.float32).reshape(ECN, 128).T)

    ident = np.eye(128, dtype=np.float32).astype(ml_dtypes.bfloat16)
    ones_bf = np.ones((128, 1), np.float32).astype(ml_dtypes.bfloat16)
    ones_row = np.ones((1, 128), np.float32)
    cb = np.tile(np.array([[EXP_SHIFT, EPS]], np.float32), (128, 1))

    j = np.arange(64)
    inv_freq = 1.0 / 10000.0 ** (2.0 * j / 128.0)

    in_maps = []
    for c in range(NCORES):
        xTc = np.ascontiguousarray(x[0, c * T:(c + 1) * T, :].T.astype(np.float32))
        t = np.arange(c * T, (c + 1) * T, dtype=np.float64)
        ang = np.outer(t, inv_freq)
        ct = np.cos(ang).T.astype(np.float32)
        st = np.sin(ang).T.astype(np.float32)
        cs2 = np.ascontiguousarray(
            np.concatenate([ct, ct], axis=0)).astype(ml_dtypes.bfloat16)
        sn2 = np.ascontiguousarray(
            np.concatenate([-st, st], axis=0)).astype(ml_dtypes.bfloat16)
        in_maps.append(dict(
            xT=xTc, wq_b=wq_b, wk_b=wk_b, wv_b=wv_b, wo_b=wo_b,
            wg_b=wg_b, wu_b=wu_b, wd_b=wd_b, sg=sg, su=su, sd=sd,
            cs2=cs2, sn2=sn2,
            ident=ident, ones_bf=ones_bf, ones_row=ones_row, cb=cb))
    return in_maps


def kernel(**inputs):
    in_maps = _prep_inputs(**inputs)
    nc = build_program()
    res = run_bass_kernel_spmd(nc, in_maps, core_ids=list(range(NCORES)))
    parts = [np.asarray(r["y_out"]) for r in res.results]
    full = np.concatenate(parts, axis=1)
    return np.ascontiguousarray(full.T)[None, :, :].astype(np.float32)


# version 21
# speedup vs baseline: 1.0539x; 1.0539x over previous
import dataclasses
import math
import numpy as np
import ml_dtypes
import concourse.bass as bass
import concourse.mybir as mybir
import concourse.bass_utils as _bu
from concourse.bass_utils import run_bass_kernel_spmd

if not getattr(_bu, "_ldw_patched", False):
    _orig_run_command = _bu.run_command

    def _run_command_ldw(cmd, *a, **kw):
        if isinstance(cmd, list):
            cmd = ["--enable-ldw-opt=true" if c == "--enable-ldw-opt=false" else c
                   for c in cmd]
        return _orig_run_command(cmd, *a, **kw)

    _bu.run_command = _run_command_ldw
    _bu._ldw_patched = True

F32 = mybir.dt.float32
BF16 = mybir.dt.bfloat16
I8 = mybir.dt.int8
AF = mybir.ActivationFunctionType
OP = mybir.AluOpType

PROG_TAG = "v21"
NCORES = 8
T = 1024
D = 2048
H = 16
DCN = 16
ICN = 64
ECN = 16
KC = 64
EXP_SHIFT = -12.0
EXP_SCALE = 1.0 / math.sqrt(128.0)
EPS = 1e-5
BATCHES = [3] * 21 + [1]
NB = len(BATCHES)
PVLAG = 3
PSLOT = 6
TRC_BI = 6

A_KH, A_VH, A_PT = 0, 32768, 65792
A_CS, A_SN, A_R1, A_R2 = 65792, 67840, 69888, 71936
A_SCR = 73984
A_QH = 78080
A_VS = 0
A_X2 = 0
A_XC = 65792
A_YSQ = 78080
A_F = 0
A_GU = 65792
A_OUT = 0
ARENA = 84224

ENGS = ("tensor", "vector", "scalar", "sync", "gpsimd")


def dap(base, off_elems, dims):
    return dataclasses.replace(
        base, offset=base.offset + off_elems, ap=[list(d) for d in dims]
    )


class Sched:
    def __init__(self, nc, sem_names):
        self.nc = nc
        self.sems = {n: nc.alloc_semaphore(n) for n in sem_names}
        self.cnt = {n: 0 for n in sem_names}
        self.ops = {e: [] for e in ENGS}

    def wait(self, eng, sem, val):
        if val is None or val <= 0:
            return
        s = self.sems[sem]
        self.ops[eng].append(lambda e, s=s, v=val: e.wait_ge(s, v))

    def emit(self, eng, fn, inc=None, incby=1):
        if inc is None:
            self.ops[eng].append(fn)
            return None
        s = self.sems[inc]
        self.cnt[inc] += incby
        tgt = self.cnt[inc]
        self.ops[eng].append(lambda e, fn=fn, s=s, b=incby: fn(e).then_inc(s, b))
        return tgt

    def emit_m(self, eng, fn, incs):
        tgts = {}
        for name in incs:
            self.cnt[name] += 1
            tgts[name] = self.cnt[name]
        s0 = self.sems[incs[0]]
        rest = [self.sems[n] for n in incs[1:]]

        def wrapped(e, fn=fn, s0=s0, rest=rest):
            inst = fn(e).then_inc(s0, 1)
            for s in rest:
                e.nop().then_inc(s, 1)
            return inst
        self.ops[eng].append(wrapped)
        return tgts

    def dma(self, q, out, in_, inc, **kw):
        return self.emit(
            q, lambda e, o=out, i=in_, kw=kw: e.dma_start(out=o, in_=i, **kw),
            inc=inc, incby=16)

    def run(self, eng_name, eng):
        for fn in self.ops[eng_name]:
            fn(eng)


def build_program(nphases=10, dbg=None):
    nc = bass.Bass("TRN2", target_bir_lowering=False, debug=False)

    xT_h = nc.dram_tensor("xT", [D, T], F32, kind="ExternalInput")
    wq_b = nc.dram_tensor("wq_b", [H, DCN, 128, 128], F32, kind="ExternalInput").ap()
    wk_b = nc.dram_tensor("wk_b", [H, DCN, 128, 128], F32, kind="ExternalInput").ap()
    wv_b = nc.dram_tensor("wv_b", [4, DCN, 128, 512], F32, kind="ExternalInput").ap()
    wo_b = nc.dram_tensor("wo_b", [ECN, DCN, 128, 128], F32, kind="ExternalInput").ap()
    wg_b = nc.dram_tensor("wg_b", [ICN, DCN, 128, 128], I8, kind="ExternalInput").ap()
    wu_b = nc.dram_tensor("wu_b", [ICN, DCN, 128, 128], I8, kind="ExternalInput").ap()
    wd_b = nc.dram_tensor("wd_b", [ECN, ICN, 128, 128], I8, kind="ExternalInput").ap()
    sg_i = nc.dram_tensor("sg", [128, ICN], F32, kind="ExternalInput").ap()
    su_i = nc.dram_tensor("su", [128, ICN], F32, kind="ExternalInput").ap()
    sd_i = nc.dram_tensor("sd", [128, ECN], F32, kind="ExternalInput").ap()
    cs_i = nc.dram_tensor("cs2", [128, T], BF16, kind="ExternalInput").ap()
    sn_i = nc.dram_tensor("sn2", [128, T], BF16, kind="ExternalInput").ap()
    id_i = nc.dram_tensor("ident", [128, 128], BF16, kind="ExternalInput").ap()
    ob_i = nc.dram_tensor("ones_bf", [128, 1], BF16, kind="ExternalInput").ap()
    or_i = nc.dram_tensor("ones_row", [1, 128], F32, kind="ExternalInput").ap()
    cb_i = nc.dram_tensor("cb", [128, 2], F32, kind="ExternalInput").ap()
    y_out_h = nc.dram_tensor("y_out", [D, T], F32, kind="ExternalOutput")
    dbgst = (nc.dram_tensor("dbg", [3, 1024], F32, kind="ExternalOutput")
             if dbg == "stats" else None)

    kv_loc = nc.dram_tensor("kv_loc", [4 * T, T], BF16, kind="Internal")
    kv_all = nc.dram_tensor("kv_all", [NCORES * 4 * T, T], BF16, kind="Internal",
                            addr_space="Shared")
    q_spill = nc.dram_tensor("q_spill", [D, T], BF16, kind="Internal")
    x2_spill = nc.dram_tensor("x2_spill", [D, T], BF16, kind="Internal")
    row_spill = nc.dram_tensor("row_spill", [1, 3072], F32, kind="Internal")
    xT = xT_h.ap()

    s_xb = nc.alloc_sbuf_tensor("s_xb", [128, DCN * T], BF16)
    s_xn = nc.alloc_sbuf_tensor("s_xn", [128, DCN * T], BF16)
    s_kot = nc.alloc_sbuf_tensor("s_kot", [128, H * T], BF16)
    s_wb = nc.alloc_sbuf_tensor("s_wb", [128, 2 * 4096], BF16)
    s_id = nc.alloc_sbuf_tensor("s_id", [128, 128], BF16)
    s_sg = nc.alloc_sbuf_tensor("s_sg", [128, ICN], F32)
    s_su = nc.alloc_sbuf_tensor("s_su", [128, ICN], F32)
    s_sd = nc.alloc_sbuf_tensor("s_sd", [128, ECN], F32)
    s_ones = nc.alloc_sbuf_tensor("s_ones", [128, 1], BF16)
    s_or = nc.alloc_sbuf_tensor("s_or", [1, 128], F32)
    s_cb = nc.alloc_sbuf_tensor("s_cb", [128, 2], F32)
    s_rows = nc.alloc_sbuf_tensor("s_rows", [1, 3072], F32)
    s_fl = nc.alloc_sbuf_tensor("s_fl", [128, 8], BF16)
    a_rs = nc.alloc_sbuf_tensor("a_rs", [128, 12], F32)
    nc.alloc_sbuf_tensor("tag_" + PROG_TAG, [1, 2], BF16)

    base, _aend = nc.bump_sbuf(ARENA)

    def mk_at(name, off, shape, dtype):
        return nc.alloc_sbuf_tensor_at(name, shape, dtype, offset=base + off)

    a_kh = mk_at("a_kh", A_KH, [128, 2 * 8192], BF16)
    a_vh = mk_at("a_vh", A_VH, [128, 2 * 8256], BF16)
    a_pt = mk_at("a_pt", A_PT, [128, PSLOT * 1536], BF16)
    a_cs = mk_at("a_cs", A_CS, [128, T], BF16)
    a_sn = mk_at("a_sn", A_SN, [128, T], BF16)
    a_r1 = mk_at("a_r1", A_R1, [128, T], BF16)
    a_r2 = mk_at("a_r2", A_R2, [128, T], BF16)
    a_scr = mk_at("a_scr", A_SCR, [128, T], F32)
    a_qh = mk_at("a_qh", A_QH, [128, 2 * T], BF16)
    a_vs = mk_at("a_vs", A_VS, [128, 8 * D], BF16)
    a_x2 = mk_at("a_x2", A_X2, [128, DCN * T], F32)
    a_xc = mk_at("a_xc", A_XC, [128, 2 * T], F32)
    a_ysq = mk_at("a_ysq", A_YSQ, [128, 2 * T], BF16)
    a_f = mk_at("a_f", A_F, [128, ICN * 512], BF16)
    a_gu = mk_at("a_gu", A_GU, [128, 4 * 512], BF16)
    a_out = mk_at("a_out", A_OUT, [128, 2 * T], F32)
    a_xc4 = mk_at("a_xc4", 0, [128, 4 * T], BF16)
    a_out4 = mk_at("a_out4", 16384, [128, 4 * T], F32)
    a_mb = mk_at("a_mb", 65792, [128, T], BF16)
    a_rb = mk_at("a_rb", 67840, [128, T], BF16)

    ps = nc.alloc_psum_tensor("ps", [128, 4096], F32)

    sem_names = [
        "c", "xb", "w0", "w1", "w2", "w3", "pw", "kst", "cck", "ccv",
        "kh0", "kh1", "vh0", "vh1", "qld0", "qld1", "qst0", "qst1",
        "s", "p", "pv", "an", "tr", "vc",
        "v", "a", "pe", "xc0", "xc1", "sp", "la",
        "x2l0", "x2l1", "x2l2", "x2l3", "yst0", "yst1", "yst2", "yst3",
        "dbg",
    ]
    S = Sched(nc, sem_names)

    wstate = {"wt": 0, "fam_start": 0, "nslots": 2}

    def dims_dst(src_dims):
        out = []
        stride = 1
        counts = [d[1] for d in src_dims[1:]]
        for cnt in reversed(counts):
            out.insert(0, [stride, cnt])
            stride *= cnt
        return out

    def wfamily(nslots):
        wstate["fam_start"] = wstate["wt"]
        wstate["nslots"] = nslots

    def wload(src_ap, dims, off_elems):
        wt = wstate["wt"]
        wstate["wt"] = wt + 1
        n = wstate["nslots"]
        fs = wstate["fam_start"]
        slot = (wt - fs) % n
        off = slot * (8192 // n)
        sem = "w%d" % slot
        gate = wt - n + 1 if wt - n >= fs else fs
        S.wait("gpsimd", "pw", gate)
        dst = dap(s_wb.ap(), off, [[8192, 128]] + dims_dst(dims))
        src = dap(src_ap, off_elems, dims)
        tgt = S.dma("gpsimd", dst, src, sem)
        return off, sem, tgt

    def bank(i, lo=0, hi=512):
        return ps[:, 512 * i + lo: 512 * i + hi]

    def bankrow(i, lo, hi):
        return ps[0:1, 512 * i + lo: 512 * i + hi]

    S.dma("sync", a_cs[:, :], cs_i, "c")
    S.dma("sync", a_sn[:, :], sn_i, "c")
    S.dma("sync", s_id[:, :], id_i, "c")
    S.dma("sync", s_sg[:, :], sg_i, "c")
    S.dma("sync", s_su[:, :], su_i, "c")
    S.dma("sync", s_sd[:, :], sd_i, "c")
    S.dma("sync", s_ones[:, :], ob_i, "c")
    S.dma("sync", s_or[:, :], or_i, "c")
    S.dma("sync", s_cb[:, :], cb_i, "c")
    NCONST = S.cnt["c"]
    for q4 in range(4):
        S.dma("gpsimd",
              s_xb[:, q4 * 4 * T:(q4 + 1) * 4 * T].rearrange(
                  "p (dc t) -> p dc t", dc=4),
              dap(xT, q4 * 4 * 128 * T, [[T, 128], [128 * T, 4], [1, T]]),
              "xb")
    ones_ap = dap(a_vh.ap(), 128, [[2 * 8256, 128], [8256, 2], [129, 64], [1, 1]])

    def pe_spacer(dummy_lo):
        return S.emit("tensor", lambda e: e.matmul(
            ps[0:1, dummy_lo:dummy_lo + 512], s_ones[:, :], s_xb[:, 0:512],
            start=True, stop=True), inc="pe")

    def launder_rows(c0, c1, producer):
        S.wait("sync", producer[0], producer[1])
        t1 = S.dma("sync", row_spill[0:1, c0:c1], s_rows[0:1, c0:c1], "la")
        S.wait("sync", "la", t1)
        return S.dma("sync", s_rows[0:1, c0:c1], row_spill[0:1, c0:c1], "la")

    def ln_finalize_bcast(sx_lo, sx2_lo, dummy_lo, dbg_rows=False):
        sp_t = pe_spacer(dummy_lo)
        S.wait("scalar", "pe", sp_t)
        if dbg_rows:
            for i, lo in ((0, sx_lo), (1, sx2_lo)):
                v_t0 = S.emit("vector", lambda e, lo=lo: e.tensor_copy(
                    a_scr[0:1, 0:1024], ps[0:1, lo:lo + 1024]), inc="v")
                S.wait("sync", "v", v_t0)
                d_t0 = S.dma("sync", dbgst[i:i + 1, :], a_scr[0:1, 0:1024], "dbg")
                S.wait("vector", "dbg", d_t0)
        S.emit("scalar", lambda e: e.activation(
            s_rows[0:1, 0:1024], ps[0:1, sx_lo:sx_lo + 1024],
            AF.Copy, scale=1.0 / 2048.0), inc="a")
        a_t = S.emit("scalar", lambda e: e.activation(
            s_rows[0:1, 1024:2048], ps[0:1, sx2_lo:sx2_lo + 1024],
            AF.Copy, scale=1.0 / 2048.0), inc="a")
        la1 = launder_rows(0, 2048, ("a", a_t))
        S.wait("vector", "la", la1)
        def fv(e):
            e.tensor_mul(s_rows[0:1, 2048:3072], s_rows[0:1, 0:1024],
                         s_rows[0:1, 0:1024])
            return e.tensor_sub(s_rows[0:1, 1024:2048], s_rows[0:1, 1024:2048],
                                s_rows[0:1, 2048:3072])
        v_t = S.emit("vector", fv, inc="v")
        la2 = launder_rows(1024, 2048, ("v", v_t))
        S.wait("scalar", "la", la2)
        def g(e):
            e.activation(s_rows[0:1, 2048:3072], s_rows[0:1, 1024:2048], AF.Ln,
                         bias=s_cb[0:1, 1:2])
            return e.activation(s_rows[0:1, 2048:3072], s_rows[0:1, 2048:3072],
                                AF.Exp, scale=-0.5)
        a_t2 = S.emit("scalar", g, inc="a")
        la3 = launder_rows(2048, 3072, ("a", a_t2))
        S.wait("tensor", "la", la3)
        pe_t = None
        for half in range(2):
            S.emit("tensor", lambda e, h=half: e.matmul(
                bank(4 + h), s_or[:, :], s_rows[0:1, h * 512:(h + 1) * 512],
                start=True, stop=True))
        for half in range(2):
            pe_t = S.emit("tensor", lambda e, h=half: e.matmul(
                bank(6 + h), s_or[:, :],
                s_rows[0:1, 2048 + h * 512: 2048 + (h + 1) * 512],
                start=True, stop=True), inc="pe" if half == 1 else None)
        return pe_t

    def mbrb_copy(bc_t):
        S.wait("vector", "pe", bc_t)
        S.emit("vector", lambda e: e.tensor_copy(a_mb[:, :], ps[:, 2048:3072]),
               inc="v")
        S.emit("vector", lambda e: e.tensor_copy(a_rb[:, :], ps[:, 3072:4096]),
               inc="v")
        return S.emit("vector", lambda e: e.tensor_copy(
            s_fl[:, 0:8], a_rb[:, T - 8:T]), inc="v")

    def ln_stats(src_t, first_war=None, xb_gate=False):
        sq_pe = {}
        for dc in range(DCN):
            if xb_gate and dc % 4 == 0:
                S.wait("vector", "xb", 16 * (dc // 4 + 1))
                if dc == 0:
                    S.wait("tensor", "xb", 16)
            if dc >= 2:
                S.wait("vector", "pe", sq_pe[dc - 2])
            S.emit("vector", lambda e, dc=dc: e.tensor_mul(
                a_ysq[:, (dc % 2) * T:((dc % 2) + 1) * T],
                src_t[:, dc * T:(dc + 1) * T], src_t[:, dc * T:(dc + 1) * T]),
                inc="v")
            sq_v = S.emit("vector", lambda e, dc=dc: e.tensor_copy(
                s_fl[:, 0:8],
                a_ysq[:, (dc % 2) * T + T - 8:((dc % 2) + 1) * T]), inc="v")
            S.wait("tensor", "v", sq_v)
            if dc == 0 and first_war is not None:
                S.wait("tensor", first_war[0], first_war[1])
            if xb_gate and dc % 4 == 0 and dc > 0:
                S.wait("tensor", "xb", 16 * (dc // 4 + 1))
            for half in range(2):
                S.emit("tensor", lambda e, dc=dc, h=half: e.matmul(
                    bankrow(h, 0, 512), s_ones[:, :],
                    src_t[:, dc * T + h * 512: dc * T + (h + 1) * 512],
                    start=(dc == 0), stop=(dc == DCN - 1)))
            for half in range(2):
                sq_pe[dc] = S.emit("tensor", lambda e, dc=dc, h=half: e.matmul(
                    bankrow(2 + h, 0, 512), s_ones[:, :],
                    a_ysq[:, (dc % 2) * T + h * 512:((dc % 2) * T + (h + 1) * 512)],
                    start=(dc == 0), stop=(dc == DCN - 1)), inc="pe")
        return sq_pe

    XN1_V = None
    if nphases >= 1:
        S.wait("vector", "c", NCONST)
        S.wait("tensor", "c", NCONST)
        S.wait("scalar", "c", NCONST)
        sq_pe = ln_stats(s_xb, xb_gate=True)
        S.wait("vector", "pe", sq_pe[DCN - 1])
        bc_t = ln_finalize_bcast(0, 1024, 2048)
        S.wait("vector", "pe", bc_t)
        S.emit("vector", lambda e: e.tensor_copy(a_scr[0:1, 0:1024], ps[0:1, 3072:4096]))
        XN1_DC = {}
        for dc in range(DCN):
            S.emit("vector", lambda e, dc=dc: e.tensor_sub(
                a_scr[:, :], s_xb[:, dc * T:(dc + 1) * T], ps[:, 2048:3072]))
            XN1_V = S.emit("vector", lambda e, dc=dc: e.tensor_mul(
                s_xn[:, dc * T:(dc + 1) * T], a_scr[:, :], ps[:, 3072:4096]),
                inc="v")
            XN1_DC[dc] = XN1_V

    def emit_rope(buf, col_off):
        A = buf[:, col_off:col_off + T]
        Alo = buf[0:64, col_off:col_off + T]
        Ahi = buf[64:128, col_off:col_off + T]
        def f(e):
            e.tensor_copy(a_r1[0:64, :], Ahi)
            e.tensor_copy(a_r1[64:128, :], Alo)
            e.tensor_mul(a_r2[:, :], A, a_cs[:, :])
            e.tensor_mul(a_r1[:, :], a_r1[:, :], a_sn[:, :])
            return e.tensor_add(A, a_r2[:, :], a_r1[:, :])
        return S.emit("vector", f, inc="v")

    def proj_ws(n_oc, wsrc_ap, out_cb, first_war=None, after_wload=None,
                npairs=2):
        evac_t = {}
        for oc in range(n_oc):
            slot, wsem, wtgt = wload(
                wsrc_ap, [[128, 128], [128 * 128, DCN], [1, 128]],
                oc * (DCN * 128 * 128))
            if after_wload is not None and oc == 1:
                after_wload()
            pair = (oc % npairs) * 2
            S.wait("tensor", wsem, wtgt)
            if oc == 0 and first_war is not None:
                S.wait("tensor", first_war[0], first_war[1])
            if oc == 1:
                S.wait("tensor", "v", XN1_V)
            if oc >= npairs:
                S.wait("tensor", "a", evac_t[oc - npairs])
            for half in range(2):
                for dc in range(DCN):
                    last = (half == 1) and (dc == DCN - 1)
                    if oc == 0 and half == 0:
                        S.wait("tensor", "v", XN1_DC[dc])
                    S.emit("tensor", lambda e, s=slot, dc=dc, h=half, p=pair: e.matmul(
                        bank(p + h),
                        s_wb[:, s + dc * 128: s + (dc + 1) * 128],
                        s_xn[:, dc * T + h * 512: dc * T + (h + 1) * 512],
                        start=(dc == 0), stop=(dc == DCN - 1)),
                        inc="pw" if last else None)
            S.wait("scalar", "pw", S.cnt["pw"])
            evac_t[oc] = out_cb(oc, pair)
        return evac_t

    if nphases >= 2:
        wfamily(4)
        def k_out(oc, pair):
            a_t = S.emit("scalar", lambda e, oc=oc, p=pair: e.activation(
                s_kot[:, oc * T:(oc + 1) * T], ps[:, p * 512:(p + 2) * 512], AF.Copy),
                inc="a")
            S.wait("vector", "a", a_t)
            emit_rope(s_kot, oc * T)
            return a_t
        proj_ws(H, wk_b, k_out)
        KROPE_V = S.cnt["v"]
        S.wait("sync", "v", KROPE_V)
        S.dma("sync",
              dap(kv_loc.ap(), 0, [[T, 128], [128 * T, H], [1, T]]),
              s_kot.ap().rearrange("p (hc t) -> p hc t", hc=H),
              "kst")

    if nphases >= 3:
        wfamily(4)
        vev_t = {}
        vst_t = {}
        for dg in range(4):
            vslots = []
            for qt in range(4):
                sl, sm, tg = wload(
                    wv_b, [[512, 128], [128 * 512, 4], [1, 512]],
                    (dg * DCN + qt * 4) * (128 * 512))
                vslots.append((sl, sm, tg))
            for tc in range(8):
                key = dg * 8 + tc
                vpair = 4 + (tc % 2)
                if key >= 2:
                    S.wait("tensor", "a", vev_t[key - 2])
                pe_done = None
                for dc in range(DCN):
                    if tc == 0 and dc % 4 == 0:
                        S.wait("tensor", vslots[dc // 4][1], vslots[dc // 4][2])
                    slot = vslots[dc // 4][0]
                    i4 = dc % 4
                    mmfn = (lambda e, s=slot, i4=i4, dc=dc, tc=tc, p=vpair:
                            e.matmul(
                                bank(p),
                                s_xn[:, dc * T + tc * 128: dc * T + (tc + 1) * 128],
                                s_wb[:, s + i4 * 512:(s + (i4 + 1) * 512)],
                                start=(dc == 0), stop=(dc == DCN - 1)))
                    pw_inc = (tc == 7) and (dc % 4 == 3)
                    if dc == DCN - 1:
                        incs = ["pe"] + (["pw"] if pw_inc else [])
                        pe_done = S.emit_m("tensor", mmfn, incs)["pe"]
                    else:
                        S.emit("tensor", mmfn, inc="pw" if pw_inc else None)
                S.wait("scalar", "pe", pe_done)
                vev_t[key] = S.emit("scalar", lambda e, tc=tc, dg=dg, p=vpair: e.activation(
                    a_vs[:, tc * D + dg * 512: tc * D + (dg + 1) * 512],
                    bank(p), AF.Copy), inc="a")
                if dg == 3:
                    S.wait("sync", "a", vev_t[key])
                    kvflat = kv_loc.ap().rearrange("a b -> (a b)")
                    vst_t[tc] = S.dma(
                        "sync",
                        dap(kvflat, D * T + tc * 16384,
                            [[128, 128], [T * 128, H], [1, 128]]),
                        a_vs[:, tc * D:(tc + 1) * D].rearrange(
                            "p (h d) -> p h d", h=H),
                        "kst")
        KST_ALL = S.cnt["kst"]

    if nphases >= 4:
        a_before_q = S.cnt["a"]
        trig = {"done": False}

        def maybe_trigger():
            if trig["done"]:
                return
            trig["done"] = True
            S.wait("gpsimd", "kst", KST_ALL)
            S.emit("gpsimd", lambda e: e.collective_compute(
                "AllGather", OP.bypass, replica_groups=[list(range(NCORES))],
                ins=[kv_loc.ap()], outs=[kv_all.ap()]), inc="cck")

        def q_out(oc, pair):
            if oc >= 2:
                S.wait("scalar", "qst%d" % (oc % 2), 16 * (oc // 2))
            a_t = S.emit("scalar", lambda e, oc=oc, p=pair: e.activation(
                a_qh[:, (oc % 2) * T:((oc % 2) + 1) * T],
                ps[:, p * 512:(p + 2) * 512], AF.Copy), inc="a")
            S.wait("vector", "a", a_t)
            r_t = emit_rope(a_qh, (oc % 2) * T)
            S.wait("sync", "v", r_t)
            S.dma("sync", q_spill[oc * 128:(oc + 1) * 128, :],
                  a_qh[:, (oc % 2) * T:((oc % 2) + 1) * T], "qst%d" % (oc % 2))
            return a_t
        wfamily(4)
        proj_ws(H, wq_b, q_out, first_war=("a", a_before_q),
                after_wload=maybe_trigger, npairs=3)

    VC_ALL = None
    if nphases >= 5:
        A_Q_END = S.cnt["a"]
        V_Q_END = S.cnt["v"]
        ONES_V = S.emit("vector", lambda e, a=ones_ap: e.memset(a, 1.0), inc="v")
        exp_tgt, pv_tgt, an_tgt, tr_tgt, vc_tgt = {}, {}, {}, {}, {}
        vc_half = {}
        kh_tgt, vh_tgt, qld_tgt = {}, {}, {}
        upv_last = {}
        kvallk = kv_all.ap()
        kvall_flat = kvallk.rearrange("a b -> (a b)")

        QON = 2 * T

        def o_cols(qs):
            return (6 * 512 + qs * 130) if qs < 3 else (7 * 512)

        batches = []
        g = 0
        for u in range(32):
            cbase = 0
            for bi, w in enumerate(BATCHES):
                batches.append((u, bi, w, cbase, g))
                cbase += w
                g += 1
        NG = g
        ginfo = {bg[4]: bg for bg in batches}

        def emit_pv(g2):
            u2, bi2, w2, cbase2, _ = ginfo[g2]
            h2 = u2 // 2
            grp2 = g2 % PSLOT
            S.wait("tensor", "p", exp_tgt[g2])
            if bi2 == 0:
                if u2 == 0:
                    S.wait("tensor", "v", ONES_V)
                if u2 % 2 == 0:
                    S.wait("tensor", "vh%d" % (h2 % 2), vh_tgt[h2])
            for i in range(w2):
                j = cbase2 + i
                for qs in range(4):
                    if bi2 == 0 and i == 0 and u2 >= 1:
                        if qs == 0:
                            S.wait("tensor", "vc", vc_half[u2 - 1])
                        elif qs == 3:
                            S.wait("tensor", "vc", vc_tgt[u2 - 1])
                    last = (i == w2 - 1) and (qs == 3)
                    st = (j == 0) and (qs in (0, 3))
                    sp = (j == KC - 1) and (qs in (2, 3))
                    S.emit("tensor",
                           lambda e, h=h2, pt=grp2, i=i, j=j, qs=qs, st=st, sp=sp:
                           e.matmul(
                               ps[:, o_cols(qs): o_cols(qs) + 129],
                               a_pt[:, pt * 1536 + i * 512 + qs * 128:
                                    (pt * 1536 + i * 512 + (qs + 1) * 128)],
                               a_vh[:, (h % 2) * 8256 + j * 129:
                                    ((h % 2) * 8256 + (j + 1) * 129)],
                               start=st, stop=sp, skip_group_check=True),
                           inc="pv" if last else None)
            pv_tgt[g2] = S.cnt["pv"]
            if bi2 == NB - 1:
                upv_last[u2] = pv_tgt[g2]

        ev_bounce = {}

        def emit_evac1(uev):
            S.wait("scalar", "pv", upv_last[uev])
            if uev >= 1:
                S.wait("scalar", "tr", tr_tgt[uev - 1])
            def evac1(e):
                for qs in range(4):
                    e.activation(a_rs[:, qs:qs + 1],
                                 ps[:, o_cols(qs) + 128: o_cols(qs) + 129], AF.Ln)
                e.activation(a_rs[:, 4:8], a_rs[:, 0:4], AF.Exp, scale=-1.0)
                e.activation(a_rs[:, 8:12], a_rs[:, 4:8], AF.Copy)
                return e.activation(a_rs[:, 0:4], a_rs[:, 8:12], AF.Copy)
            ev_bounce[uev] = S.emit("scalar", evac1, inc="a")

        def emit_evac2(uev):
            S.wait("vector", "a", ev_bounce[uev])
            for qs in range(4):
                an_tgt[uev] = S.emit("vector", lambda e, qs=qs: e.tensor_scalar_mul(
                    s_xn[:, QON + qs * 128: QON + (qs + 1) * 128],
                    ps[:, o_cols(qs): o_cols(qs) + 128],
                    a_rs[:, 8 + qs:9 + qs]), inc="an")

        TRT = {0: 3072, 1: 3200, 2: 3584, 3: 3712}

        def emit_trcopy(pu):
            S.wait("tensor", "an", an_tgt[pu])
            if pu >= 1:
                S.wait("tensor", "vc", vc_tgt[pu - 1])
            tr_half = None
            for qs in range(4):
                t = S.emit("tensor", lambda e, qs=qs: e.matmul(
                    ps[:, TRT[qs]: TRT[qs] + 128],
                    s_xn[:, QON + qs * 128: QON + (qs + 1) * 128], s_id[:, :],
                    start=(qs in (0, 2)), stop=True, skip_group_check=True),
                    inc="tr" if qs in (1, 3) else None)
                if qs == 1:
                    tr_half = t
                elif qs == 3:
                    tr_tgt[pu] = t
            ph_, pq = pu // 2, pu % 2
            S.wait("vector", "tr", tr_half)
            for qs in range(4):
                if qs == 2:
                    S.wait("vector", "tr", tr_tgt[pu])
                t = S.emit("vector", lambda e, ph=ph_, pq=pq, qs=qs: e.tensor_copy(
                    s_kot[:, ph * T + pq * 512 + qs * 128:
                          (ph * T + pq * 512 + (qs + 1) * 128)],
                    ps[:, TRT[qs]: TRT[qs] + 128]),
                    inc="vc" if qs in (1, 3) else None)
                if qs == 1:
                    vc_half[pu] = t
                elif qs == 3:
                    vc_tgt[pu] = t

        for (u, bi, w, cbase, g) in batches:
            h, qh = u // 2, u % 2
            if bi == 0 and qh == 0:
                b = h % 2
                if h >= 2:
                    S.wait("sync", "pv", upv_last[2 * h - 3])
                S.wait("sync", "qst%d" % b, 16 * (h // 2 + 1))
                qld_tgt[h] = S.dma(
                    "sync", s_xn[:, b * T:(b + 1) * T],
                    q_spill[h * 128:(h + 1) * 128, :], "qld%d" % b)
                if h < 2:
                    S.wait("sync", "cck", 1)
                kh_tgt[h] = S.dma(
                    "sync",
                    a_kh[:, b * 8192:(b + 1) * 8192].rearrange("p (r t) -> p r t", r=8),
                    dap(kvallk, h * 128 * T,
                        [[T, 128], [4 * T * T, NCORES], [1, T]]),
                    "kh%d" % b)
                if h < 2:
                    S.wait("sync", "v", ONES_V)
                for r in range(NCORES):
                    vh_tgt[h] = S.dma(
                        "sync",
                        dap(a_vh.ap(), b * 8256 + r * 1032,
                            [[2 * 8256, 128], [129, 8], [1, 128]]),
                        dap(kvall_flat, r * 4 * T * T + D * T + h * T * 128,
                            [[128, 128], [16384, 8], [1, 128]]),
                        "vh%d" % b)

            grp = g % 2
            ptile = g % PSLOT
            if g == 0:
                S.wait("tensor", "a", A_Q_END)
            if g >= 2:
                S.wait("tensor", "p", exp_tgt[g - 2])
            if bi == 0 and qh == 0:
                S.wait("tensor", "kh%d" % (h % 2), kh_tgt[h])
                S.wait("tensor", "qld%d" % (h % 2), qld_tgt[h])
            for i in range(w):
                j = cbase + i
                S.emit("tensor", lambda e, h=h, qh=qh, j=j, grp=grp, i=i: e.matmul(
                    ps[:, grp * 1536 + i * 512: grp * 1536 + (i + 1) * 512],
                    a_kh[:, (h % 2) * 8192 + j * 128:((h % 2) * 8192 + (j + 1) * 128)],
                    s_xn[:, (h % 2) * T + qh * 512:((h % 2) * T + (qh + 1) * 512)],
                    start=True, stop=True),
                    inc="s" if i == w - 1 else None)
            s_tgt_g = S.cnt["s"]
            if g - PVLAG >= 0:
                u3e, bi3e = ginfo[g - PVLAG][0], ginfo[g - PVLAG][1]
                if bi3e == 0 and u3e >= 1:
                    emit_evac2(u3e - 1)
            if bi == TRC_BI and u >= 1:
                emit_trcopy(u - 1)
            if PVLAG <= bi < TRC_BI and u >= 1:
                pass
            elif bi == TRC_BI and u >= 1:
                for x in range(g - TRC_BI, g - PVLAG + 1):
                    emit_pv(x)
            elif g >= PVLAG:
                emit_pv(g - PVLAG)
            if g == 0:
                S.wait("scalar", "v", V_Q_END)
            S.wait("scalar", "s", s_tgt_g)
            if g >= PSLOT:
                S.wait("scalar", "pv", pv_tgt[g - PSLOT])
            exp_tgt[g] = S.emit("scalar", lambda e, grp=grp, pt=ptile, w=w: e.activation(
                a_pt[:, pt * 1536: pt * 1536 + w * 512],
                ps[:, grp * 1536: grp * 1536 + w * 512],
                AF.Exp, bias=s_cb[:, 0:1], scale=EXP_SCALE), inc="p")
            uev = None
            if g - PVLAG >= 0:
                u3, bi3 = ginfo[g - PVLAG][0], ginfo[g - PVLAG][1]
                if bi3 == NB - 1:
                    uev = u3
            if uev is not None:
                emit_evac1(uev)

        for g2 in range(NG - PVLAG, NG):
            emit_pv(g2)
        emit_evac1(31)
        emit_evac2(31)
        emit_trcopy(31)
        VC_ALL = S.cnt["vc"]

    if nphases >= 6:
        wfamily(4)
        S.wait("sync", "pv", S.cnt["pv"])
        X2LD = S.dma("sync",
                     a_x2.ap().rearrange("p (dc t) -> p dc t", dc=DCN),
                     dap(xT, 0, [[T, 128], [128 * T, DCN], [1, T]]),
                     "xc0")
        ev_t = {}
        ostat_pe_t = {}
        for ec in range(ECN):
            slot, wsem, wtgt = wload(
                wo_b, [[128, 128], [128 * 128, DCN], [1, 128]],
                ec * (DCN * 128 * 128))
            pair = (ec % 2) * 2
            S.wait("tensor", wsem, wtgt)
            if ec == 0:
                S.wait("tensor", "vc", VC_ALL)
            if ec >= 2:
                S.wait("tensor", "v", ev_t[ec - 2])
            for half in range(2):
                for dc in range(DCN):
                    last = half == 1 and dc == DCN - 1
                    S.emit("tensor", lambda e, s=slot, dc=dc, h=half, p=pair: e.matmul(
                        bank(p + h),
                        s_wb[:, s + dc * 128: s + (dc + 1) * 128],
                        s_kot[:, dc * T + h * 512: dc * T + (h + 1) * 512],
                        start=(dc == 0), stop=(dc == DCN - 1)),
                        inc="pw" if last else None)
            S.wait("vector", "pw", S.cnt["pw"])
            if ec >= 2:
                S.wait("vector", "pe", ostat_pe_t[ec - 2])
            S.emit("vector", lambda e, ec=ec, p=pair: e.tensor_copy(
                s_xb[:, ec * T:(ec + 1) * T], ps[:, p * 512:(p + 2) * 512]), inc="v")
            S.emit("vector", lambda e, ec=ec: e.tensor_mul(
                a_ysq[:, (ec % 2) * T:((ec % 2) + 1) * T],
                s_xb[:, ec * T:(ec + 1) * T], s_xb[:, ec * T:(ec + 1) * T]), inc="v")
            sq_v = S.emit("vector", lambda e, ec=ec: e.tensor_copy(
                s_fl[:, 0:8],
                a_ysq[:, (ec % 2) * T + T - 8:((ec % 2) + 1) * T]), inc="v")
            ev_t[ec] = sq_v
            S.wait("tensor", "v", sq_v)
            for half in range(2):
                S.emit("tensor", lambda e, ec=ec, h=half: e.matmul(
                    bankrow(4 + h, 0, 512), s_ones[:, :],
                    s_xb[:, ec * T + h * 512: ec * T + (h + 1) * 512],
                    start=(ec == 0), stop=(ec == ECN - 1)))
            for half in range(2):
                ostat_pe_t[ec] = S.emit("tensor", lambda e, ec=ec, h=half: e.matmul(
                    bankrow(6 + h, 0, 512), s_ones[:, :],
                    a_ysq[:, (ec % 2) * T + h * 512:((ec % 2) * T + (h + 1) * 512)],
                    start=(ec == 0), stop=(ec == ECN - 1)),
                    inc="pe" if half == 1 else None)
        OSTAT_PE = S.cnt["pe"]
        S.wait("vector", "pe", OSTAT_PE)
        bc_t = ln_finalize_bcast(2048, 3072, 0, dbg_rows=(dbg == "stats"))
        if dbg == "stats":
            v_t2 = S.emit("vector", lambda e: e.tensor_copy(
                a_scr[0:1, 0:1024], s_rows[0:1, 1024:2048]), inc="v")
            S.wait("sync", "v", v_t2)
            S.dma("sync", dbgst[2:3, :], a_scr[0:1, 0:1024], "dbg")
        mbrb_copy(bc_t)
        x2v_t = {}
        ln3_pe = {}
        for dc in range(DCN):
            S.wait("vector", "xc0", X2LD)
            def f2(e, dc=dc):
                e.tensor_sub(a_scr[:, :], s_xb[:, dc * T:(dc + 1) * T], a_mb[:, :])
                e.tensor_mul(a_scr[:, :], a_scr[:, :], a_rb[:, :])
                return e.tensor_add(s_xb[:, dc * T:(dc + 1) * T],
                                    a_x2[:, dc * T:(dc + 1) * T], a_scr[:, :])
            x2v_t[dc] = S.emit("vector", f2, inc="v")
            S.emit("vector", lambda e, dc=dc: e.tensor_copy(
                s_fl[:, 0:8], s_xb[:, (dc + 1) * T - 8:(dc + 1) * T]), inc="v")
            if dc >= 2:
                S.wait("vector", "pe", ln3_pe[dc - 2])
            S.emit("vector", lambda e, dc=dc: e.tensor_mul(
                a_ysq[:, (dc % 2) * T:((dc % 2) + 1) * T],
                s_xb[:, dc * T:(dc + 1) * T], s_xb[:, dc * T:(dc + 1) * T]),
                inc="v")
            sq3_v = S.emit("vector", lambda e, dc=dc: e.tensor_copy(
                s_fl[:, 0:8],
                a_ysq[:, (dc % 2) * T + T - 8:((dc % 2) + 1) * T]), inc="v")
            S.wait("tensor", "v", sq3_v)
            for half in range(2):
                S.emit("tensor", lambda e, dc=dc, h=half: e.matmul(
                    bankrow(h, 0, 512), s_ones[:, :],
                    s_xb[:, dc * T + h * 512: dc * T + (h + 1) * 512],
                    start=(dc == 0), stop=(dc == DCN - 1)))
            for half in range(2):
                ln3_pe[dc] = S.emit("tensor", lambda e, dc=dc, h=half: e.matmul(
                    bankrow(2 + h, 0, 512), s_ones[:, :],
                    a_ysq[:, (dc % 2) * T + h * 512:((dc % 2) * T + (h + 1) * 512)],
                    start=(dc == 0), stop=(dc == DCN - 1)), inc="pe")

    XN3_V = None
    if nphases >= 7:
        S.wait("vector", "pe", ln3_pe[DCN - 1])
        bc_t = ln_finalize_bcast(0, 1024, 2048)
        mbrb_copy(bc_t)
        XN3_DC = {}
        for dc in range(DCN):
            S.emit("vector", lambda e, dc=dc: e.tensor_sub(
                a_ysq[:, 0:T], s_xb[:, dc * T:(dc + 1) * T], a_mb[:, :]))
            XN3_V = S.emit("vector", lambda e, dc=dc: e.tensor_mul(
                s_xn[:, dc * T:(dc + 1) * T], a_ysq[:, 0:T], a_rb[:, :]),
                inc="v")
            XN3_DC[dc] = XN3_V
        S.wait("sync", "v", XN3_V)
        S.dma("sync",
              dap(x2_spill.ap(), 0, [[T, 128], [128 * T, DCN], [1, T]]),
              s_xb.ap().rearrange("p (dc t) -> p dc t", dc=DCN), "sp")

    dstat_v = {}
    if nphases >= 8:
        fuse_v = {}
        for th in range(2):
            wfamily(4)
            for ic in range(ICN):
                gslot, gsem, gtgt = wload(
                    wg_b, [[128, 128], [128 * 128, DCN], [1, 128]],
                    ic * (DCN * 128 * 128))
                uslot, usem, utgt = wload(
                    wu_b, [[128, 128], [128 * 128, DCN], [1, 128]],
                    ic * (DCN * 128 * 128))
                gb = (ic % 4) * 2
                S.wait("tensor", gsem, gtgt)
                if th == 0 and ic == 1:
                    S.wait("tensor", "v", XN3_V)
                if th == 1 and ic == 0:
                    S.wait("tensor", "v", dstat_v[0])
                if ic >= 4:
                    S.wait("tensor", "v", fuse_v[(th, ic - 4)])
                for dc in range(DCN):
                    if th == 0 and ic == 0:
                        S.wait("tensor", "v", XN3_DC[dc])
                    S.emit("tensor", lambda e, s=gslot, dc=dc, th=th, b=gb: e.matmul(
                        bank(b),
                        s_wb[:, s + dc * 128: s + (dc + 1) * 128],
                        s_xn[:, dc * T + th * 512: dc * T + (th + 1) * 512],
                        start=(dc == 0), stop=(dc == DCN - 1)),
                        inc="pw" if dc == DCN - 1 else None)
                S.wait("tensor", usem, utgt)
                for dc in range(DCN):
                    S.emit("tensor", lambda e, s=uslot, dc=dc, th=th, b=gb: e.matmul(
                        bank(b + 1),
                        s_wb[:, s + dc * 128: s + (dc + 1) * 128],
                        s_xn[:, dc * T + th * 512: dc * T + (th + 1) * 512],
                        start=(dc == 0), stop=(dc == DCN - 1)),
                        inc="pw" if dc == DCN - 1 else None)
                pw_now = S.cnt["pw"]
                S.wait("scalar", "pw", pw_now)
                if ic >= 2:
                    S.wait("scalar", "v", fuse_v[(th, ic - 2)])
                S.emit("scalar", lambda e, ic=ic, b=gb: e.activation(
                    a_gu[:, (ic % 2) * 512:((ic % 2) + 1) * 512],
                    bank(b), AF.Silu, scale=s_sg[:, ic:ic + 1]), inc="a")
                silu_t = S.emit("scalar", lambda e, ic=ic: e.activation(
                    s_fl[:, 0:8],
                    a_gu[:, (ic % 2) * 512 + 504:((ic % 2) + 1) * 512], AF.Copy),
                    inc="a")
                S.wait("vector", "a", silu_t)
                S.emit("vector", lambda e, ic=ic, b=gb: e.tensor_scalar_mul(
                    a_gu[:, 1024 + (ic % 2) * 512: 1024 + ((ic % 2) + 1) * 512],
                    bank(b + 1), s_su[:, ic:ic + 1]))
                if th == 0 and ic == 0:
                    S.wait("vector", "sp", 16)
                S.emit("vector", lambda e, ic=ic: e.tensor_mul(
                    a_f[:, ic * 512:(ic + 1) * 512],
                    a_gu[:, (ic % 2) * 512:((ic % 2) + 1) * 512],
                    a_gu[:, 1024 + (ic % 2) * 512: 1024 + ((ic % 2) + 1) * 512]),
                    inc="v")
                fuse_v[(th, ic)] = S.emit("vector", lambda e, ic=ic: e.tensor_copy(
                    s_fl[:, 0:8], a_f[:, ic * 512 + 504:(ic + 1) * 512]), inc="v")
            wfamily(2)
            ev_v = {}
            for ec in range(ECN):
                yb = 4 + (ec % 2)
                if ec == 0:
                    S.wait("tensor", "v", fuse_v[(th, ICN - 1)])
                if ec >= 2:
                    S.wait("tensor", "v", ev_v[ec - 2])
                for halfw in range(2):
                    slot, wsem, wtgt = wload(
                        wd_b, [[128, 128], [128 * 128, 32], [1, 128]],
                        (ec * ICN + halfw * 32) * (128 * 128))
                    S.wait("tensor", wsem, wtgt)
                    for i32 in range(32):
                        j = halfw * 32 + i32
                        S.emit("tensor", lambda e, s=slot, i32=i32, j=j, b=yb: e.matmul(
                            bank(b),
                            s_wb[:, s + i32 * 128: s + (i32 + 1) * 128],
                            a_f[:, j * 512:(j + 1) * 512],
                            start=(j == 0), stop=(j == ICN - 1)),
                            inc="pw" if i32 == 31 else None)
                pw_now = S.cnt["pw"]
                S.wait("vector", "pw", pw_now)
                S.emit("vector", lambda e, ec=ec, b=yb, th=th: e.tensor_scalar_mul(
                    s_xb[:, ec * T + th * 512: ec * T + (th + 1) * 512],
                    bank(b), s_sd[:, ec:ec + 1]), inc="v")
                S.emit("vector", lambda e, ec=ec, th=th: e.tensor_mul(
                    a_ysq[:, (ec % 2) * 512:((ec % 2) + 1) * 512],
                    s_xb[:, ec * T + th * 512: ec * T + (th + 1) * 512],
                    s_xb[:, ec * T + th * 512: ec * T + (th + 1) * 512]), inc="v")
                ev_v[ec] = S.emit("vector", lambda e, ec=ec: e.tensor_copy(
                    s_fl[:, 0:8],
                    a_ysq[:, (ec % 2) * 512 + 504:((ec % 2) + 1) * 512]), inc="v")
                S.wait("tensor", "v", ev_v[ec])
                S.emit("tensor", lambda e, ec=ec, th=th: e.matmul(
                    bankrow(0, 0, 512), s_ones[:, :],
                    s_xb[:, ec * T + th * 512: ec * T + (th + 1) * 512],
                    start=(ec == 0), stop=(ec == ECN - 1)))
                S.emit("tensor", lambda e, ec=ec: e.matmul(
                    bankrow(1, 0, 512), s_ones[:, :],
                    a_ysq[:, (ec % 2) * 512:((ec % 2) + 1) * 512],
                    start=(ec == 0), stop=(ec == ECN - 1)),
                    inc="pe" if ec == ECN - 1 else None)
            sp_t = pe_spacer(1024)
            S.wait("scalar", "pe", sp_t)
            S.emit("scalar", lambda e, th=th: e.activation(
                s_rows[0:1, th * 512:(th + 1) * 512],
                bankrow(0, 0, 512), AF.Copy), inc="a")
            a_t = S.emit("scalar", lambda e, th=th: e.activation(
                s_rows[0:1, 1024 + th * 512: 1024 + (th + 1) * 512],
                bankrow(1, 0, 512), AF.Copy), inc="a")
            S.wait("vector", "a", a_t)
            dstat_v[th] = S.emit("vector", lambda e: e.engine_nop(), inc="v")
            if th == 1:
                DOWN_END_PE = S.cnt["pe"]

    if nphases >= 9:
        S.wait("sync", "v", dstat_v[1])
        la0 = launder_rows(0, 2048, ("a", S.cnt["a"]))
        S.wait("vector", "la", la0)
        def f3(e):
            e.tensor_scalar_mul(s_rows[0:1, 0:1024], s_rows[0:1, 0:1024], 1.0 / 2048.0)
            e.tensor_scalar_mul(s_rows[0:1, 1024:2048], s_rows[0:1, 1024:2048],
                                1.0 / 2048.0)
            e.tensor_mul(s_rows[0:1, 2048:3072], s_rows[0:1, 0:1024],
                         s_rows[0:1, 0:1024])
            return e.tensor_sub(s_rows[0:1, 1024:2048], s_rows[0:1, 1024:2048],
                                s_rows[0:1, 2048:3072])
        v_t = S.emit("vector", f3, inc="v")
        la1 = launder_rows(0, 2048, ("v", v_t))
        S.wait("scalar", "la", la1)
        def g3(e):
            e.activation(s_rows[0:1, 2048:3072], s_rows[0:1, 1024:2048], AF.Ln,
                         bias=s_cb[0:1, 1:2])
            return e.activation(s_rows[0:1, 2048:3072], s_rows[0:1, 2048:3072],
                                AF.Exp, scale=-0.5)
        a_t = S.emit("scalar", g3, inc="a")
        la2 = launder_rows(2048, 3072, ("a", a_t))
        S.wait("tensor", "la", la2)
        for half in range(2):
            S.emit("tensor", lambda e, h=half: e.matmul(
                bank(4 + h), s_or[:, :], s_rows[0:1, h * 512:(h + 1) * 512],
                start=True, stop=True))
        for half in range(2):
            bc_t = S.emit("tensor", lambda e, h=half: e.matmul(
                bank(6 + h), s_or[:, :],
                s_rows[0:1, 2048 + h * 512: 2048 + (h + 1) * 512],
                start=True, stop=True), inc="pe" if half == 1 else None)
        mbrb_copy(bc_t)
        out_v = {}
        S.wait("scalar", "pe", DOWN_END_PE)
        for dc in range(DCN):
            b = dc % 4
            if dc >= 4:
                S.wait("scalar", "v", out_v[dc - 4])
            x2l_t = S.dma("scalar", a_xc4[:, b * T:(b + 1) * T],
                          x2_spill[dc * 128:(dc + 1) * 128, :], "x2l%d" % b)
            S.wait("vector", "x2l%d" % b, x2l_t)
            if dc >= 4:
                S.wait("vector", "yst%d" % b, 16 * (dc // 4))
            def f4(e, dc=dc, b=b):
                e.tensor_sub(a_ysq[:, 0:T], s_xb[:, dc * T:(dc + 1) * T], a_mb[:, :])
                e.tensor_mul(a_ysq[:, 0:T], a_ysq[:, 0:T], a_rb[:, :])
                return e.tensor_add(a_out4[:, b * T:(b + 1) * T],
                                    a_xc4[:, b * T:(b + 1) * T], a_ysq[:, 0:T])
            S.emit("vector", f4, inc="v")
            out_v[dc] = S.emit("vector", lambda e, b=b: e.tensor_copy(
                s_fl[:, 0:2], a_out4[:, (b + 1) * T - 2:(b + 1) * T]), inc="v")
            S.wait("sync", "v", out_v[dc])
            S.dma("sync", y_out_h[dc * 128:(dc + 1) * 128, :],
                  a_out4[:, b * T:(b + 1) * T], "yst%d" % b)

    dbg_src = None
    if dbg == "xn":
        dbg_t = nc.dram_tensor("dbg", [128, DCN * T], BF16, kind="ExternalOutput").ap()
        dbg_src = s_xn[:, :]
    elif dbg == "kot":
        dbg_t = nc.dram_tensor("dbg", [128, H * T], BF16, kind="ExternalOutput").ap()
        dbg_src = s_kot[:, :]
    elif dbg == "xb":
        dbg_t = nc.dram_tensor("dbg", [128, DCN * T], BF16, kind="ExternalOutput").ap()
        dbg_src = s_xb[:, :]
    elif dbg == "x2":
        dbg_t = nc.dram_tensor("dbg", [128, DCN * T], BF16, kind="ExternalOutput").ap()
        dbg_src = s_xb[:, :]
    elif dbg == "kvall":
        dbg_t = nc.dram_tensor("dbg", [NCORES * 4 * T, T], BF16,
                               kind="ExternalOutput").ap()
        dbg_src = kv_all.ap()
    elif dbg == "qspill":
        dbg_t = nc.dram_tensor("dbg", [D, T], BF16, kind="ExternalOutput").ap()
        dbg_src = q_spill.ap()


    for name in sem_names:
        if name == "dbg" or S.cnt[name] == 0:
            continue
        S.wait("sync", name, S.cnt[name])
    if dbg_src is not None:
        S.dma("sync", dbg_t, dbg_src, "dbg")
        S.wait("sync", "dbg", 16)

    with nc.Block() as block:
        @block.tensor
        def _(e):
            S.run("tensor", e)

        @block.vector
        def _(e):
            S.run("vector", e)

        @block.scalar
        def _(e):
            S.run("scalar", e)

        @block.sync
        def _(e):
            S.run("sync", e)

        @block.gpsimd
        def _(e):
            S.run("gpsimd", e)

    return nc



def _prep_inputs(L0_attn_x, L0_attn_wq, L0_attn_wk, L0_attn_wv, L0_attn_wo,
                 L0_ffn_w_gate, L0_ffn_w_up, L0_ffn_w_down,
                 L0_ffn_s_gate, L0_ffn_s_up, L0_ffn_s_down):
    x = np.asarray(L0_attn_x)
    wq = np.asarray(L0_attn_wq, dtype=np.float32)
    wk = np.asarray(L0_attn_wk, dtype=np.float32)
    wv = np.asarray(L0_attn_wv, dtype=np.float32)
    wo = np.asarray(L0_attn_wo, dtype=np.float32)
    wg = np.asarray(L0_ffn_w_gate)
    wu = np.asarray(L0_ffn_w_up)
    wd = np.asarray(L0_ffn_w_down)

    deint = np.concatenate([np.arange(0, 128, 2), np.arange(1, 128, 2)])
    perm = np.concatenate([h * 128 + deint for h in range(H)])

    def blk_oc(w_lay):
        return np.ascontiguousarray(
            w_lay.reshape(DCN, 128, w_lay.shape[1] // 128, 128).transpose(2, 0, 1, 3))

    wq_b = blk_oc(wq.T[:, perm])
    wk_b = blk_oc(wk.T[:, perm])
    wo_b = blk_oc(wo.T)
    wv_b = np.ascontiguousarray(
        wv.T.reshape(DCN, 128, 4, 512).transpose(2, 0, 1, 3))
    wg_b = np.ascontiguousarray(wg.reshape(DCN, 128, ICN, 128).transpose(2, 0, 1, 3))
    wu_b = np.ascontiguousarray(wu.reshape(DCN, 128, ICN, 128).transpose(2, 0, 1, 3))
    wd_b = np.ascontiguousarray(wd.reshape(ICN, 128, ECN, 128).transpose(2, 0, 1, 3))

    sg = np.ascontiguousarray(np.asarray(L0_ffn_s_gate, np.float32).reshape(ICN, 128).T)
    su = np.ascontiguousarray(np.asarray(L0_ffn_s_up, np.float32).reshape(ICN, 128).T)
    sd = np.ascontiguousarray(np.asarray(L0_ffn_s_down, np.float32).reshape(ECN, 128).T)

    ident = np.eye(128, dtype=np.float32).astype(ml_dtypes.bfloat16)
    ones_bf = np.ones((128, 1), np.float32).astype(ml_dtypes.bfloat16)
    ones_row = np.ones((1, 128), np.float32)
    cb = np.tile(np.array([[EXP_SHIFT, EPS]], np.float32), (128, 1))

    j = np.arange(64)
    inv_freq = 1.0 / 10000.0 ** (2.0 * j / 128.0)

    in_maps = []
    for c in range(NCORES):
        xTc = np.ascontiguousarray(x[0, c * T:(c + 1) * T, :].T.astype(np.float32))
        t = np.arange(c * T, (c + 1) * T, dtype=np.float64)
        ang = np.outer(t, inv_freq)
        ct = np.cos(ang).T.astype(np.float32)
        st = np.sin(ang).T.astype(np.float32)
        cs2 = np.ascontiguousarray(
            np.concatenate([ct, ct], axis=0)).astype(ml_dtypes.bfloat16)
        sn2 = np.ascontiguousarray(
            np.concatenate([-st, st], axis=0)).astype(ml_dtypes.bfloat16)
        in_maps.append(dict(
            xT=xTc, wq_b=wq_b, wk_b=wk_b, wv_b=wv_b, wo_b=wo_b,
            wg_b=wg_b, wu_b=wu_b, wd_b=wd_b, sg=sg, su=su, sd=sd,
            cs2=cs2, sn2=sn2,
            ident=ident, ones_bf=ones_bf, ones_row=ones_row, cb=cb))
    return in_maps


def kernel(**inputs):
    in_maps = _prep_inputs(**inputs)
    nc = build_program()
    res = run_bass_kernel_spmd(nc, in_maps, core_ids=list(range(NCORES)))
    parts = [np.asarray(r["y_out"]) for r in res.results]
    full = np.concatenate(parts, axis=1)
    return np.ascontiguousarray(full.T)[None, :, :].astype(np.float32)
